# version 14
# baseline (speedup 1.0000x reference)
import math
import numpy as np
import ml_dtypes
from contextlib import ExitStack

import concourse.bass as bass
import concourse.mybir as mybir
from concourse.bass_utils import run_bass_kernel_spmd

F32 = mybir.dt.float32
BF16 = mybir.dt.bfloat16
AF = mybir.ActivationFunctionType
ALU = mybir.AluOpType

NEG = -1.0e30
E = 2048
NI = 16
H = 16
C = 512
TC = 2048
TA = 2560
NTC = 16
NTA = 20
SCALE = 1.0 / math.sqrt(128.0)
ENGS = ("sync", "tensor", "vector", "scalar", "gpsimd")


class Sem:
    def __init__(self, nc, stack, name):
        self.h = stack.enter_context(nc.semaphore(name))
        self.n = 0


class Prog:

    def __init__(self):
        self.streams = {e: [] for e in ENGS}

    def emit(self, eng, fn):
        self.streams[eng].append(fn)

    def wait(self, eng, sem, tgt):
        if tgt > 0:
            self.emit(eng, lambda e, s=sem.h, t=tgt: e.wait_ge(s, t))

    def op(self, eng, builder, sem=None, k=1):
        if sem is None:
            self.emit(eng, builder)
            return None
        sem.n += k
        self.emit(eng, lambda e, b=builder, s=sem.h, kk=k: b(e).then_inc(s, kk))
        return sem.n

    def dma(self, eng, out_ap, in_ap, sem):
        sem.n += 16
        self.emit(eng, lambda e, o=out_ap, i=in_ap, s=sem.h: e.dma_start(o, i).then_inc(s, 16))
        return sem.n

    def barrier(self, bar):
        for e in ENGS:
            self.emit(e, lambda en, s=bar.h: en.nop().then_inc(s, 1))
        bar.n += len(ENGS)
        tgt = bar.n
        for e in ENGS:
            self.emit(e, lambda en, s=bar.h, t=tgt: en.wait_ge(s, t))


def build(dbg=frozenset(), phases=5):
    nc = bass.Bass("TRN2", target_bir_lowering=False)

    def dram(name, shape, dtype):
        kind = "ExternalOutput" if name in dbg else "Internal"
        return nc.dram_tensor(name, shape, dtype, kind=kind)

    xh = nc.dram_tensor("xh", [TA, E], F32, kind="ExternalInput").ap()
    wqkvT = nc.dram_tensor("wqkvT", [E, 3 * E], F32, kind="ExternalInput").ap()
    woutT = nc.dram_tensor("woutT", [E, E], F32, kind="ExternalInput").ap()
    wffnT = nc.dram_tensor("wffnT", [E, E], F32, kind="ExternalInput").ap()
    bias1 = nc.dram_tensor("bias1", [1, E], F32, kind="ExternalInput").ap()
    identN = nc.dram_tensor("identN", [128, 128], BF16, kind="ExternalInput").ap()
    ones_in = nc.dram_tensor("ones_in", [128, 512], BF16, kind="ExternalInput").ap()
    triP_in = nc.dram_tensor("triP_in", [128, 256], BF16, kind="ExternalInput").ap()
    triO_in = nc.dram_tensor("triO_in", [128, 256], BF16, kind="ExternalInput").ap()
    m0row = nc.dram_tensor("m0row", [1, 128], BF16, kind="ExternalInput").ap()

    out = nc.dram_tensor("out", [TC, E], F32, kind="ExternalOutput").ap()

    QT = dram("QT", [H, 128, TC], BF16).ap()
    KT = dram("KT", [H, 128, TA], BF16).ap()
    VN = dram("VN", [NTA, 128, E], BF16).ap()
    OT = dram("OT", [H, 128, TC], BF16).ap()
    G = dram("G", [NTC, 128, E], BF16).ap()
    if "XNT" in dbg:
        XNTd = nc.dram_tensor("XNT", [NI, 128, TA], BF16, kind="ExternalOutput").ap()

    prog = Prog()

    with ExitStack() as outer:
        EO = outer.enter_context
        ident = EO(nc.sbuf_tensor("ident", [128, 128], BF16))
        ones_sb = EO(nc.sbuf_tensor("ones_sb", [128, 512], BF16))
        triP01 = EO(nc.sbuf_tensor("triP01", [128, 256], BF16))
        triO01 = EO(nc.sbuf_tensor("triO01", [128, 256], BF16))
        m0_sb = EO(nc.sbuf_tensor("m0_sb", [1, 128], BF16))
        bias_sb = EO(nc.sbuf_tensor("bias_sb", [1, E], BF16))

        SM = lambda name: Sem(nc, outer, name)
        bar = SM("bar")
        const_ld = SM("const_ld")
        bconst_ld = SM("bconst_ld")
        CONST_TOT = 5 * 16

        prog.dma("sync", ident[:], identN, const_ld)
        prog.dma("sync", ones_sb[:], ones_in, const_ld)
        prog.dma("sync", triP01[:], triP_in, const_ld)
        prog.dma("sync", triO01[:], triO_in, const_ld)
        prog.dma("sync", m0_sb[:], m0row, const_ld)
        prog.dma("gpsimd", bias_sb[:], bias1, bconst_ld)

        abc = EO(nc.sbuf_tensor("abc", [128, E], BF16))
        s123 = ExitStack()
        xnt = s123.enter_context(nc.sbuf_tensor("xnt", [128, NI * TA], BF16))
        xnt3 = xnt[:].rearrange("p (i t) -> p i t", i=NI)
        s23 = ExitStack()
        wsl = []
        slab_ld = [SM("slab_ld0"), SM("slab_ld1")]
        slab_free = {}
        slab_tgt = {}

        def load_slab(s):
            if s >= 2:
                sem, cnt = slab_free[s - 2]
                prog.wait("gpsimd", sem, cnt)
            slab_tgt[s] = prog.dma(
                "gpsimd", wsl[s % 2][:],
                wqkvT[:, 512 * s:512 * (s + 1)].rearrange("(i p) e -> p i e", p=128),
                slab_ld[s % 2])

        NCH = 20
        with ExitStack() as p12:
            E1 = p12.enter_context
            wffn = E1(nc.sbuf_tensor("wffn", [128, NI * E], BF16))
            gst = [E1(nc.sbuf_tensor(f"gst{p}", [128, E], BF16)) for p in range(2)]
            sf = [E1(nc.sbuf_tensor(f"sf{p}", [128, 512], F32)) for p in range(2)]
            arow = E1(nc.sbuf_tensor("arow", [1, E], BF16))
            biasbc = E1(nc.sbuf_tensor("biasbc", [128, E], BF16))
            g_ps = [E1(nc.psum_tensor(f"g_ps{p}", [128, 512], F32)) for p in range(2)]
            tp_ps = [E1(nc.psum_tensor(f"tp_ps{p}", [128, 512], F32)) for p in range(2)]
            al_ps = E1(nc.psum_tensor("al_ps", [1, 4 * 512], F32))
            abst = E1(nc.sbuf_tensor("abst", [128, E], BF16))
            EL = E1
            xbuf = [EL(nc.sbuf_tensor(f"xbuf{p}", [128, E], F32)) for p in range(2)]
            xnat = [EL(nc.sbuf_tensor(f"xnat{p}", [128, E], BF16)) for p in range(2)]
            stats = EL(nc.sbuf_tensor("stats", [128, 24], F32))
            mv = [EL(nc.sbuf_tensor(f"mv{p}", [128, 2], F32)) for p in range(2)]
            veps = [EL(nc.sbuf_tensor(f"veps{p}", [128, 1], F32)) for p in range(2)]
            sqv = [EL(nc.sbuf_tensor(f"sqv{p}", [128, 1], F32)) for p in range(2)]
            rstd = [EL(nc.sbuf_tensor(f"rstd{p}", [128, 1], F32)) for p in range(2)]
            nmr = [EL(nc.sbuf_tensor(f"nmr{p}", [128, 1], F32)) for p in range(2)]

            x_ld = [SM(f"x_ld{p}") for p in range(2)]
            v_self = SM("v_self")
            s_self = SM("s_self")
            norm_done = SM("norm_done")
            tp_done = SM("tp_done")
            tp_evV = SM("tp_evV")
            tp_evS = SM("tp_evS")
            wff_ld = [SM(f"wff_ld{r}") for r in range(4)]
            abs_dn = SM("abs_dn")
            bin_dn = SM("bin_dn")
            almm = SM("almm")
            alev = SM("alev")
            abmm = SM("abmm")
            abev = SM("abev")
            rbias = SM("rbias")
            gmm = SM("gmm")
            gsf = SM("gsf")
            silu_dn = SM("silu_dn")
            g_st = [SM("g_st0"), SM("g_st1")]

            wffn3 = wffn[:].rearrange("p (i e) -> p i e", i=NI)
            prog.wait("tensor", const_ld, CONST_TOT)

            for c in range(NCH):
                if c >= 2:
                    prog.wait("sync", norm_done, c - 1)
                prog.dma("sync", xbuf[c % 2][:], xh[128 * c:128 * (c + 1), :],
                         x_ld[c % 2])
            for r in range(4):
                prog.dma("gpsimd", wffn3[:, 4 * r:4 * (r + 1), :],
                         wffnT[512 * r:512 * (r + 1), :].rearrange("(i p) e -> p i e", p=128),
                         wff_ld[r])

            def emit_alpha_tile(i):
                prog.wait("scalar", wff_ld[i // 4], 16)
                if i >= 1:
                    prog.wait("scalar", almm, 4 * i)
                prog.op("scalar", lambda e, i=i: e.activation(
                    abst[:], wffn[:, E * i:E * (i + 1)], AF.Abs), abs_dn)
                prog.wait("vector", abs_dn, i + 1)
                prog.op("vector", lambda e, i=i: e.tensor_scalar(
                    wffn[:, E * i:E * (i + 1)], wffn[:, E * i:E * (i + 1)],
                    0.0, 2.0, ALU.is_ge, ALU.mult), bin_dn)
                prog.wait("tensor", abs_dn, i + 1)
                for og in range(4):
                    prog.op("tensor", lambda e, i=i, og=og: e.matmul(
                        al_ps[0:1, 512 * og:512 * (og + 1)], ones_sb[:, 0:1],
                        abst[:, 512 * og:512 * (og + 1)],
                        start=(i == 0), stop=(i == NI - 1)), almm)

            def emit_alpha_finish():
                prog.wait("scalar", almm, 4 * NI)
                for og in range(4):
                    prog.op("scalar", lambda e, og=og: e.activation(
                        arow[0:1, 512 * og:512 * (og + 1)],
                        al_ps[0:1, 512 * og:512 * (og + 1)], AF.Copy,
                        scale=1.0 / E), alev)
                prog.wait("tensor", alev, 4)
                prog.wait("tensor", bconst_ld, 16)
                for g in range(8):
                    gp = g % 2
                    src = arow if g < 4 else bias_sb
                    dstb = abc if g < 4 else biasbc
                    o = (g % 4) * 512
                    if g >= 2:
                        prog.wait("tensor", abev, g - 1)
                    prog.op("tensor", lambda e, src=src, o=o, gp=gp: e.matmul(
                        g_ps[gp][:], ones_sb[0:1, 0:128], src[0:1, o:o + 512],
                        start=True, stop=True), abmm)
                    prog.wait("vector", abmm, g + 1)
                    prog.op("vector", lambda e, d=dstb, o=o, gp=gp: e.tensor_copy(
                        d[:, o:o + 512], g_ps[gp][:]), abev)

            ffn_mmq = 0

            def emit_ffn_tile(tt):
                nonlocal ffn_mmq
                tp = tt % 2
                if tt == 0:
                    prog.wait("tensor", bin_dn, NI)
                    prog.wait("tensor", abev, 8)
                prog.wait("tensor", tp_evV, 2 * (5 + tt))
                prog.wait("tensor", tp_evS, 2 * (5 + tt))
                for og in range(4):
                    gp = ffn_mmq % 2
                    if ffn_mmq >= 2:
                        prog.wait("tensor", gsf, 2 * ffn_mmq - 3)
                    for i in range(NI):
                        prog.op("tensor", lambda e, tt=tt, og=og, i=i, gp=gp: e.matmul(
                            g_ps[gp][:],
                            xnt[:, TA * i + 512 + 128 * tt: TA * i + 512 + 128 * (tt + 1)],
                            wffn[:, E * i + 512 * og: E * i + 512 * (og + 1)],
                            start=(i == 0), stop=(i == NI - 1)), gmm)
                    prog.wait("vector", gmm, gmm.n)
                    if ffn_mmq >= 2:
                        prog.wait("vector", silu_dn, ffn_mmq - 1)
                    t = prog.op("vector", lambda e, og=og, gp=gp: e.tensor_tensor(
                        sf[gp][:], g_ps[gp][:], abc[:, 512 * og:512 * (og + 1)],
                        ALU.mult), gsf)
                    prog.wait("vector", gsf, t)
                    t = prog.op("vector", lambda e, og=og, gp=gp: e.tensor_tensor(
                        sf[gp][:], sf[gp][:], biasbc[:, 512 * og:512 * (og + 1)],
                        ALU.add), gsf)
                    prog.wait("scalar", gsf, t)
                    if og == 0 and tt >= 2:
                        prog.wait("scalar", g_st[tp], 16 * (tt // 2))
                    prog.op("scalar", lambda e, og=og, tp=tp, gp=gp: e.activation(
                        gst[tp][:, 512 * og:512 * (og + 1)], sf[gp][:], AF.Silu), silu_dn)
                    ffn_mmq += 1
                prog.wait("scalar", silu_dn, silu_dn.n)
                prog.dma("scalar", G[tt], gst[tp][:], g_st[tp])

            def ln_compute(c):
                p = c % 2
                prog.wait("vector", x_ld[c % 2], 16 * (c // 2 + 1))
                if c >= 2:
                    prog.wait("vector", norm_done, c - 1)
                for g4 in range(4):
                    t = prog.op("vector", lambda e, g=g4, p=p: e.bn_stats(
                        stats[:, 6 * g:6 * (g + 1)],
                        xbuf[p][:, 512 * g: 512 * (g + 1)]), v_self)
                prog.wait("vector", v_self, t)
                t = prog.op("vector", lambda e, p=p: e.bn_aggr(mv[p][:], stats[:]), v_self)
                prog.wait("vector", v_self, t)
                t = prog.op("vector", lambda e, p=p: e.tensor_scalar(
                    veps[p][:], mv[p][:, 1:2], 1e-5, None, ALU.add), v_self)
                prog.wait("scalar", v_self, t)
                t = prog.op("scalar", lambda e, p=p: e.activation(
                    sqv[p][:], veps[p][:], AF.Sqrt), s_self)
                prog.wait("vector", s_self, t)
                t = prog.op("vector", lambda e, p=p: e.reciprocal(rstd[p][:], sqv[p][:]), v_self)
                prog.wait("vector", v_self, t)
                tn = prog.op("vector", lambda e, p=p: e.scalar_tensor_tensor(
                    nmr[p][:], mv[p][:, 0:1], -1.0, rstd[p][:], ALU.mult, ALU.mult), v_self)
                prog.wait("scalar", v_self, tn)
                if c >= 2:
                    prog.wait("scalar", tp_done, 16 * (c - 1))
                prog.op("scalar", lambda e, p=p: e.activation(
                    xnat[p][:], xbuf[p][:], AF.Identity,
                    bias=nmr[p][:], scale=rstd[p][:]), norm_done)

            def tp_emit(c):
                p = c % 2
                prog.wait("tensor", norm_done, c + 1)
                for g in range(4):
                    gn = 4 * c + g
                    gpar = gn % 2
                    if gn >= 2:
                        prog.wait("tensor", tp_evV if gpar == 0 else tp_evS, gn // 2)
                    for kk in range(4):
                        i = 4 * g + kk
                        prog.op("tensor", lambda e, i=i, kk=kk, p=p, gp=gpar: e.matmul(
                            tp_ps[gp][:, 128 * kk:128 * (kk + 1)],
                            xnat[p][:, 128 * i: 128 * (i + 1)],
                            ident[:], start=True, stop=True), tp_done)
                    tgt = tp_done.n
                    prog.wait("vector" if gpar == 0 else "scalar", tp_done, tgt)
                    dst = xnt3[:, 4 * g:4 * g + 4, 128 * c: 128 * (c + 1)]
                    if gpar == 0:
                        prog.op("vector", lambda e, d=dst, gp=gpar: e.tensor_copy(
                            d, tp_ps[gp][:].rearrange("p (i t) -> p i t", i=4)), tp_evV)
                    else:
                        prog.op("scalar", lambda e, d=dst, gp=gpar: e.copy(
                            d, tp_ps[gp][:].rearrange("p (i t) -> p i t", i=4)), tp_evS)

            ffn_next = 0
            for c in range(NCH + 1):
                for i in range(NI):
                    if i // 3 + 1 == c:
                        emit_alpha_tile(i)
                if c == 7:
                    emit_alpha_finish()
                if c < NCH:
                    ln_compute(c)
                if c >= 1:
                    tp_emit(c - 1)
                burst = 0
                while (c >= 8 and ffn_next < NTC and 4 + ffn_next <= c - 1
                       and burst < 2):
                    emit_ffn_tile(ffn_next)
                    ffn_next += 1
                    burst += 1
            for tt in range(ffn_next, NTC):
                emit_ffn_tile(tt)
            if "XNT" in dbg:
                prog.wait("sync", tp_evV, tp_evV.n)
                prog.wait("sync", tp_evS, tp_evS.n)
                dmp = SM("dmp")
                for hf in range(2):
                    prog.dma("sync",
                             XNTd[8 * hf:8 * (hf + 1)].rearrange("i p t -> p i t"),
                             xnt3[:, 8 * hf:8 * (hf + 1), :], dmp)
                prog.wait("sync", dmp, dmp.n)
            gst_tot = [g_st[0].n, g_st[1].n]

        wsl.append(s23.enter_context(nc.sbuf_tensor("wsl0", [128, NI * 512], BF16)))
        wsl.append(s23.enter_context(nc.sbuf_tensor("wsl1", [128, NI * 512], BF16)))
        prog.barrier(bar)
        prog.wait("gpsimd", g_st[0], gst_tot[0])
        prog.wait("gpsimd", g_st[1], gst_tot[1])
        load_slab(0)
        load_slab(1)
        if phases < 3:
            s23.close()
            s123.close()
            finish(prog, nc)
            return nc

        with ExitStack() as p3:
            E3 = p3.enter_context
            qkst = [E3(nc.sbuf_tensor(f"qkst{p}", [128, TA], BF16)) for p in range(2)]
            vstb = [E3(nc.sbuf_tensor(f"vstb{p}", [128, 4 * 512], BF16)) for p in range(2)]
            qk_ps = [E3(nc.psum_tensor(f"qk_ps{p}", [128, 512], F32)) for p in range(2)]

            qkmm = SM("qkmm")
            qkev = SM("qkev")
            qkst_dn = [SM("qkst_dn0"), SM("qkst_dn1")]
            vmm = SM("vmm")
            vev = SM("vev")
            vst_dn = [SM("vst_dn0"), SM("vst_dn1")]

            for s in range(8):
                sp = s % 2
                is_q = s < 4
                ntg = 4 if is_q else 5
                toff = 512 if is_q else 0
                ev_eng = "scalar" if is_q else "vector"
                prog.wait("tensor", slab_ld[sp], slab_tgt[s])
                for el in range(4):
                    h = (4 * s + el) % 16
                    eg = 4 * s + el
                    stp = eg % 2
                    if eg >= 2:
                        prog.wait(ev_eng, qkst_dn[stp], 16 * (eg // 2))
                    for tg in range(ntg):
                        gp = qkmm.n % 2
                        if qkmm.n >= 2:
                            prog.wait("tensor", qkev, qkmm.n - 1)
                        for i in range(NI):
                            prog.op("tensor", lambda e, sp=sp, el=el, tg=tg, i=i, gp=gp, to=toff: e.matmul(
                                qk_ps[gp][:],
                                wsl[sp][:, 512 * i + 128 * el: 512 * i + 128 * (el + 1)],
                                xnt[:, TA * i + to + 512 * tg: TA * i + to + 512 * (tg + 1)],
                                start=(i == 0), stop=(i == NI - 1)),
                                qkmm if i == NI - 1 else None)
                        prog.wait(ev_eng, qkmm, qkmm.n)
                        if is_q:
                            prog.op("scalar", lambda e, tg=tg, stp=stp, gp=gp: e.activation(
                                qkst[stp][:, 512 * tg:512 * (tg + 1)], qk_ps[gp][:],
                                AF.Copy, scale=SCALE), qkev)
                        else:
                            prog.op("vector", lambda e, tg=tg, stp=stp, gp=gp: e.tensor_copy(
                                qkst[stp][:, 512 * tg:512 * (tg + 1)], qk_ps[gp][:]), qkev)
                    prog.wait("sync", qkev, qkev.n)
                    if is_q:
                        prog.dma("sync", QT[h], qkst[stp][:, 0:TC], qkst_dn[stp])
                    else:
                        prog.dma("sync", KT[h], qkst[stp][:, 0:TA], qkst_dn[stp])
                slab_free[s] = (qkmm, qkmm.n)
                if s + 2 < 12:
                    load_slab(s + 2)
            for s in range(8, 12):
                sp = s % 2
                g = s - 8
                prog.wait("tensor", slab_ld[sp], slab_tgt[s])
                for tq in range(5):
                    qg = 5 * g + tq
                    vp = qg % 2
                    if qg >= 2:
                        prog.wait("vector", vst_dn[vp], 16 * (qg // 2))
                    for a in range(4):
                        tau = 4 * tq + a
                        gp = vmm.n % 2
                        if vmm.n >= 2:
                            prog.wait("tensor", vev, vmm.n - 1)
                        for i in range(NI):
                            prog.op("tensor", lambda e, sp=sp, tau=tau, i=i, gp=gp: e.matmul(
                                qk_ps[gp][:],
                                xnt[:, TA * i + 128 * tau: TA * i + 128 * (tau + 1)],
                                wsl[sp][:, 512 * i:512 * (i + 1)],
                                start=(i == 0), stop=(i == NI - 1)),
                                vmm if i == NI - 1 else None)
                        prog.wait("vector", vmm, vmm.n)
                        prog.op("vector", lambda e, a=a, vp=vp, gp=gp: e.tensor_copy(
                            vstb[vp][:, 512 * a:512 * (a + 1)], qk_ps[gp][:]), vev)
                    prog.wait("sync", vev, vev.n)
                    prog.dma("sync",
                             VN[4 * tq:4 * tq + 4, :, 512 * g:512 * (g + 1)].rearrange(
                                 "a p e -> p a e"),
                             vstb[vp][:].rearrange("p (a e) -> p a e", a=4), vst_dn[vp])
                slab_free[s] = (vmm, vmm.n)
                if s + 2 < 12:
                    load_slab(s + 2)
            for sd in qkst_dn + vst_dn:
                prog.wait("sync", sd, sd.n)

        s23.close()
        s123.close()
        prog.barrier(bar)
        if phases < 4:
            finish(prog, nc)
            return nc

        with ExitStack() as p4:
            E4 = p4.enter_context
            qtb = [E4(nc.sbuf_tensor(f"qtb{p}", [128, H * 512], BF16)) for p in range(2)]
            ktw = E4(nc.sbuf_tensor("ktw", [128, H * 3 * 512], BF16))
            vnw = E4(nc.sbuf_tensor("vnw", [128, 3 * 4 * E], BF16))
            pts = [E4(nc.sbuf_tensor(f"pts{p}", [128, 8 * 1024], BF16)) for p in range(2)]
            rlb = [E4(nc.sbuf_tensor(f"rlb{p}", [128, 1024], F32)) for p in range(2)]
            rltmp = E4(nc.sbuf_tensor("rltmp", [128, 1024], F32))
            otst = E4(nc.sbuf_tensor("otst", [128, H * 512], BF16))
            otraw = [E4(nc.sbuf_tensor(f"otraw{p}", [128, 1024], BF16)) for p in range(2)]
            s4 = [E4(nc.psum_tensor(f"s4_{p}", [128, 1024], F32)) for p in range(2)]
            ob2 = E4(nc.psum_tensor("ob2", [128, 1024], F32))
            lb2 = E4(nc.psum_tensor("lb2", [128, 1024], F32))

            kv_ld = [SM("kv_ld0"), SM("kv_ld1"), SM("kv_ld2")]
            vn_ld = [SM("vn_ld0"), SM("vn_ld1"), SM("vn_ld2")]
            qt_ld = [SM("qt_ld0"), SM("qt_ld1")]
            smm = SM("smm")
            expd = SM("expd")
            mskd = SM("mskd")
            avl = SM("avl")
            lnd = SM("lnd")
            rld = SM("rld")
            oraw = SM("oraw")
            oev = SM("oev")
            ot_st = SM("ot_st")

            for sd in qkst_dn + vst_dn:
                prog.wait("sync", sd, sd.n)

            def load_kv(gc):
                sl = gc % 3
                prog.dma("sync",
                         ktw[:].rearrange("p (h r t) -> p h r t", h=H, r=3)[:, :, sl, :],
                         KT[:, :, 512 * gc:512 * (gc + 1)].rearrange("h p t -> p h t"),
                         kv_ld[sl])
                prog.dma("gpsimd",
                         vnw[:].rearrange("p (r a e) -> p r a e", r=3, a=4)[:, sl],
                         VN[4 * gc:4 * (gc + 1)].rearrange("a p e -> p a e"),
                         vn_ld[sl])
                return kv_ld[sl].n

            def load_qt(c):
                prog.dma("scalar", qtb[c % 2][:].rearrange("p (h t) -> p h t", h=H),
                         QT[:, :, 512 * c:512 * (c + 1)].rearrange("h p t -> p h t"),
                         qt_ld[c % 2])
                return qt_ld[c % 2].n

            kv_rdy = {0: load_kv(0), 1: load_kv(1)}
            qt_rdy = {0: load_qt(0)}

            for cc in range(4):
                gc = cc + 1
                if cc < 3:
                    if cc >= 1:
                        prog.wait("sync", avl, cc * 8)
                        prog.wait("gpsimd", avl, cc * 8)
                        prog.wait("scalar", avl, cc * 8)
                    kv_rdy[gc + 1] = load_kv(gc + 1)
                    qt_rdy[cc + 1] = load_qt(cc + 1)
                prog.wait("tensor", kv_ld[(gc - 1) % 3], kv_rdy[gc - 1])
                prog.wait("tensor", kv_ld[gc % 3], kv_rdy[gc])
                prog.wait("tensor", vn_ld[(gc - 1) % 3], kv_rdy[gc - 1])
                prog.wait("tensor", vn_ld[gc % 3], kv_rdy[gc])
                prog.wait("tensor", qt_ld[cc % 2], qt_rdy[cc])
                for h2 in range(H // 2):
                    gp2 = cc * 8 + h2
                    pp = h2 % 2
                    if gp2 >= 2:
                        prog.wait("scalar", avl, gp2 - 1)
                    jobs = [("own", jj, 128 * jj, 512, gc % 3, jj) for jj in range(4)] + \
                           [("prev", jj, 0, 128 * (jj + 1), (gc - 1) % 3, jj) for jj in range(4)]

                    def emit_sjob(idx, h2=h2, gp2=gp2, cc=cc, jobs=jobs, pp=pp):
                        gq = gp2 * 8 + idx
                        kind, jj, qlo, qhi, sl, jt = jobs[idx]
                        mend = (kind == "prev" and cc == 0)
                        if gq >= 2:
                            prog.wait("tensor", expd, gq - 1)
                        for a in range(2):
                            h = 2 * h2 + a
                            last = (a == 1) and not mend
                            prog.op("tensor", lambda e, h=h, jj=jj, qlo=qlo, qhi=qhi, sl=sl,
                                    gq=gq, a=a, cp=cc % 2, mend=mend: e.matmul(
                                s4[gq % 2][:, 512 * a + qlo:512 * a + qhi],
                                ktw[:, 1536 * h + 512 * sl + 128 * jj: 1536 * h + 512 * sl + 128 * (jj + 1)],
                                qtb[cp][:, 512 * h + qlo: 512 * h + qhi],
                                start=True, stop=not mend),
                                smm if last else None)
                            if mend:
                                prog.op("tensor", lambda e, qlo=qlo, qhi=qhi, gq=gq, a=a: e.matmul(
                                    s4[gq % 2][:, 512 * a + qlo:512 * a + qhi], m0_sb[0:1, :],
                                    ones_sb[0:1, 0:qhi - qlo],
                                    start=False, stop=True),
                                    smm if a == 1 else None)
                        prog.wait("scalar", smm, smm.n)
                        prog.op("scalar", lambda e, idx=idx, qlo=qlo, qhi=qhi, gq=gq, pp=pp: e.activation(
                            pts[pp][:].rearrange("p (s a w) -> p s a w", s=8, a=2)[:, idx, :, qlo:qhi],
                            s4[gq % 2][:].rearrange("p (a w) -> p a w", a=2)[:, :, qlo:qhi],
                            AF.Exp), expd)
                        prog.wait("vector", expd, expd.n)
                        tri = triO01 if kind == "own" else triP01
                        base = 8 * idx + jj
                        dst = pts[pp][:].rearrange("p (x w) -> p x w", w=128)[
                            :, base:base + 5:4, :]
                        prog.op("vector", lambda e, d=dst, m=tri: e.tensor_tensor(
                            d, d, m[:].rearrange("p (a w) -> p a w", a=2), ALU.mult), mskd)

                    def emit_avl(idx, h2=h2, gp2=gp2, jobs=jobs, pp=pp):
                        kind, jj, qlo, qhi, sl, jt = jobs[idx]
                        prog.wait("tensor", mskd, gp2 * 8 + idx + 1)
                        for a in range(2):
                            h = 2 * h2 + a
                            prog.op("tensor", lambda e, h=h, sl=sl, jt=jt, idx=idx, a=a,
                                    qlo=qlo, qhi=qhi, pp=pp: e.matmul(
                                ob2[:, 512 * a + qlo:512 * a + qhi],
                                vnw[:, 8192 * sl + 2048 * jt + 128 * h: 8192 * sl + 2048 * jt + 128 * (h + 1)],
                                pts[pp][:, 1024 * idx + 512 * a + qlo: 1024 * idx + 512 * a + qhi],
                                start=(idx == 0), stop=(idx == 7)))
                        for a in range(2):
                            prog.op("tensor", lambda e, idx=idx, a=a, qlo=qlo, qhi=qhi, pp=pp: e.matmul(
                                lb2[:, 512 * a + qlo:512 * a + qhi], ones_sb[:, 0:128],
                                pts[pp][:, 1024 * idx + 512 * a + qlo: 1024 * idx + 512 * a + qhi],
                                start=(idx == 0), stop=(idx == 7)),
                                avl if idx == 7 and a == 1 else None)

                    emit_sjob(0)
                    emit_sjob(1)
                    for idx in range(8):
                        if idx + 2 < 8:
                            emit_sjob(idx + 2)
                        if idx == 0 and gp2 >= 1:
                            prog.wait("tensor", oraw, gp2)
                            prog.wait("tensor", lnd, gp2)
                        emit_avl(idx)
                    prog.wait("vector", avl, avl.n)
                    prog.op("vector", lambda e, pp=pp: e.tensor_copy(
                        otraw[pp][:], ob2[:]), oraw)
                    prog.wait("scalar", avl, avl.n)
                    prog.op("scalar", lambda e: e.activation(
                        rltmp[:], lb2[:], AF.Ln), lnd)
                    if gp2 >= 2:
                        prog.wait("scalar", oev, gp2 - 1)
                    t = prog.op("scalar", lambda e, pp=pp: e.activation(
                        rlb[pp][:], rltmp[:], AF.Exp, scale=-1.0), rld)
                    prog.wait("vector", rld, t)
                    if cc >= 1 and h2 == 0:
                        prog.wait("vector", ot_st, 16 * cc)
                    prog.op("vector", lambda e, h2=h2, pp=pp: e.tensor_tensor(
                        otst[:, 1024 * h2:1024 * (h2 + 1)], otraw[pp][:], rlb[pp][:],
                        ALU.mult), oev)
                prog.wait("scalar", oev, oev.n)
                prog.dma("scalar", OT[:, :, 512 * cc:512 * (cc + 1)].rearrange("h p t -> p h t"),
                         otst[:].rearrange("p (h t) -> p h t", h=H), ot_st)
            prog.wait("scalar", ot_st, ot_st.n)

        prog.barrier(bar)
        if phases < 5:
            finish(prog, nc)
            return nc

        with ExitStack() as p5:
            E5 = p5.enter_context
            wout_sb = E5(nc.sbuf_tensor("wout_sb", [128, NI * E], BF16))
            otc = [E5(nc.sbuf_tensor(f"otc{p}", [128, H * 128], BF16)) for p in range(2)]
            gt = [E5(nc.sbuf_tensor(f"gt{p}", [128, E], BF16)) for p in range(2)]
            xt = [E5(nc.sbuf_tensor(f"xt{p}", [128, E], F32)) for p in range(2)]
            sfin = [E5(nc.sbuf_tensor(f"sfin{p}", [128, 512], F32)) for p in range(2)]
            obf = [E5(nc.sbuf_tensor(f"obf{p}", [128, E], F32)) for p in range(2)]
            s5_ps = [E5(nc.psum_tensor(f"s5_ps{p}", [128, 512], F32)) for p in range(2)]

            w5_ld = [SM(f"w5_ld{s}") for s in range(4)]
            t5_ld = [SM("t5_ld0"), SM("t5_ld1")]
            x5_ld = [SM("x5_ld0"), SM("x5_ld1")]
            s5mm = SM("s5mm")
            findn = SM("findn")
            outst = [SM("outst0"), SM("outst1")]
            v5_self = SM("v5_self")

            for sd in g_st:
                prog.wait("sync", sd, sd.n)

            for s in range(4):
                prog.dma("gpsimd",
                         wout_sb[:].rearrange("p (i e) -> p i e", i=NI)[:, :, 512 * s:512 * (s + 1)],
                         woutT[:, 512 * s:512 * (s + 1)].rearrange("(i p) e -> p i e", p=128),
                         w5_ld[s])

            def load_tt(tt):
                tp = tt % 2
                prog.wait("sync", ot_st, 16 * (tt // 4 + 1))
                prog.dma("sync", otc[tp][:].rearrange("p (h t) -> p h t", h=H),
                         OT[:, :, 128 * tt:128 * (tt + 1)].rearrange("h p t -> p h t"),
                         t5_ld[tp])
                prog.dma("sync", gt[tp][:], G[tt], t5_ld[tp])
                if tt >= 2:
                    prog.wait("scalar", outst[tp], 16 * (tt // 2))
                prog.dma("scalar", xt[tp][:],
                         xh[512 + 128 * tt: 512 + 128 * (tt + 1), :], x5_ld[tp])
                return t5_ld[tp].n

            rdy = {0: load_tt(0)}
            mmq = 0
            for tt in range(NTC):
                tp = tt % 2
                if tt < NTC - 1:
                    if tt >= 1:
                        prog.wait("sync", outst[(tt + 1) % 2], 16 * ((tt + 1) // 2))
                    rdy[tt + 1] = load_tt(tt + 1)
                prog.wait("tensor", t5_ld[tp], rdy[tt])
                prog.wait("tensor", x5_ld[tp], 16 * (tt // 2 + 1))
                if tt >= 2:
                    prog.wait("vector", outst[tp], 16 * (tt // 2))
                for og in range(4):
                    gp = mmq % 2
                    if tt == 0:
                        prog.wait("tensor", w5_ld[og], 16)
                    if mmq >= 2:
                        prog.wait("tensor", findn, mmq - 1)
                    for hh in range(H):
                        prog.op("tensor", lambda e, tp=tp, og=og, hh=hh, gp=gp: e.matmul(
                            s5_ps[gp][:],
                            otc[tp][:, 128 * hh:128 * (hh + 1)],
                            wout_sb[:, E * hh + 512 * og: E * hh + 512 * (og + 1)],
                            start=(hh == 0), stop=(hh == H - 1)),
                            s5mm if hh == H - 1 else None)
                    mmq += 1
                    prog.wait("vector", s5mm, mmq)
                    t = prog.op("vector", lambda e, tp=tp, og=og, gp=gp: e.tensor_tensor(
                        sfin[gp][:], s5_ps[gp][:], gt[tp][:, 512 * og:512 * (og + 1)],
                        ALU.mult), v5_self)
                    prog.wait("vector", v5_self, t)
                    prog.op("vector", lambda e, tp=tp, og=og, gp=gp: e.tensor_tensor(
                        obf[tp][:, 512 * og:512 * (og + 1)], sfin[gp][:],
                        xt[tp][:, 512 * og:512 * (og + 1)], ALU.add), findn)
                prog.wait("gpsimd", findn, findn.n)
                prog.dma("gpsimd", out[128 * tt:128 * (tt + 1), :], obf[tp][:], outst[tp])
            prog.wait("sync", outst[0], outst[0].n)
            prog.wait("sync", outst[1], outst[1].n)

        finish(prog, nc)
    return nc


def finish(prog, nc):
    with nc.Block() as block:
        for name in ENGS:
            deco = getattr(block, name)

            def runner(eng, name=name):
                for fn in prog.streams[name]:
                    fn(eng)

            deco(runner)



def _consts():
    ident = np.eye(128, dtype=ml_dtypes.bfloat16)
    ones = np.ones((128, 512), dtype=ml_dtypes.bfloat16)
    p = np.arange(128)[:, None]
    f = np.arange(128)[None, :]
    triP = np.where(p >= f, 1.0, 0.0).astype(ml_dtypes.bfloat16)
    triO = np.where(p <= f, 1.0, 0.0).astype(ml_dtypes.bfloat16)
    triP = np.ascontiguousarray(np.concatenate([triP, triP], axis=1))
    triO = np.ascontiguousarray(np.concatenate([triO, triO], axis=1))
    return ident, ones, triP, triO


def make_in_maps(x, wqkv, wout, wffn, bias):
    ident, ones, triP, triO = _consts()
    wqkvT = np.ascontiguousarray(np.asarray(wqkv, dtype=np.float32).T)
    woutT = np.ascontiguousarray(np.asarray(wout, dtype=np.float32).T)
    wffnT = np.ascontiguousarray(np.asarray(wffn, dtype=np.float32).T)
    bias1 = np.ascontiguousarray(np.asarray(bias, dtype=np.float32).reshape(1, E))
    in_maps = []
    for core in range(8):
        b, j = core // 4, core % 4
        lo = 2048 * j
        if j == 0:
            halo = np.zeros((C, E), dtype=np.float32)
            m0 = np.full((1, 128), NEG, dtype=ml_dtypes.bfloat16)
        else:
            halo = np.asarray(x[b, lo - C:lo], dtype=np.float32)
            m0 = np.zeros((1, 128), dtype=ml_dtypes.bfloat16)
        xhc = np.ascontiguousarray(
            np.concatenate([halo, np.asarray(x[b, lo:lo + TC], dtype=np.float32)], 0))
        in_maps.append(dict(
            xh=xhc, wqkvT=wqkvT, woutT=woutT, wffnT=wffnT, bias1=bias1,
            identN=ident, ones_in=ones, triP_in=triP, triO_in=triO, m0row=m0))
    return in_maps


def kernel(L0_attn_x, L0_attn_w_qkv, L0_attn_w_out, L0_ffn_weight, L0_ffn_bias):
    x = np.asarray(L0_attn_x)
    in_maps = make_in_maps(x, L0_attn_w_qkv, L0_attn_w_out, L0_ffn_weight, L0_ffn_bias)
    nc = build()
    res = run_bass_kernel_spmd(nc, in_maps, core_ids=list(range(8)))
    outa = np.empty((2, 8192, 2048), dtype=np.float32)
    for core in range(8):
        b, j = core // 4, core % 4
        outa[b, 2048 * j:2048 * (j + 1)] = res.results[core]["out"]
    return outa


# version 15
# speedup vs baseline: 1.0437x; 1.0437x over previous
import math
import numpy as np
import ml_dtypes
from contextlib import ExitStack

import concourse.bass as bass
import concourse.mybir as mybir
from concourse.bass_utils import run_bass_kernel_spmd

F32 = mybir.dt.float32
BF16 = mybir.dt.bfloat16
AF = mybir.ActivationFunctionType
ALU = mybir.AluOpType

NEG = -1.0e30
E = 2048
NI = 16
H = 16
C = 512
TC = 2048
TA = 2560
NTC = 16
NTA = 20
SCALE = 1.0 / math.sqrt(128.0)
ENGS = ("sync", "tensor", "vector", "scalar", "gpsimd")


class Sem:
    def __init__(self, nc, stack, name):
        self.h = stack.enter_context(nc.semaphore(name))
        self.n = 0


class Prog:

    def __init__(self):
        self.streams = {e: [] for e in ENGS}

    def emit(self, eng, fn):
        self.streams[eng].append(fn)

    def wait(self, eng, sem, tgt):
        if tgt > 0:
            self.emit(eng, lambda e, s=sem.h, t=tgt: e.wait_ge(s, t))

    def op(self, eng, builder, sem=None, k=1):
        if sem is None:
            self.emit(eng, builder)
            return None
        sem.n += k
        self.emit(eng, lambda e, b=builder, s=sem.h, kk=k: b(e).then_inc(s, kk))
        return sem.n

    def dma(self, eng, out_ap, in_ap, sem):
        sem.n += 16
        self.emit(eng, lambda e, o=out_ap, i=in_ap, s=sem.h: e.dma_start(o, i).then_inc(s, 16))
        return sem.n

    def barrier(self, bar):
        for e in ENGS:
            self.emit(e, lambda en, s=bar.h: en.nop().then_inc(s, 1))
        bar.n += len(ENGS)
        tgt = bar.n
        for e in ENGS:
            self.emit(e, lambda en, s=bar.h, t=tgt: en.wait_ge(s, t))


def build(dbg=frozenset(), phases=5):
    nc = bass.Bass("TRN2", target_bir_lowering=False)

    def dram(name, shape, dtype):
        kind = "ExternalOutput" if name in dbg else "Internal"
        return nc.dram_tensor(name, shape, dtype, kind=kind)

    xh = nc.dram_tensor("xh", [TA, E], F32, kind="ExternalInput").ap()
    wqkvT = nc.dram_tensor("wqkvT", [E, 3 * E], F32, kind="ExternalInput").ap()
    woutT = nc.dram_tensor("woutT", [E, E], F32, kind="ExternalInput").ap()
    wffnT = nc.dram_tensor("wffnT", [E, E], F32, kind="ExternalInput").ap()
    bias1 = nc.dram_tensor("bias1", [1, E], F32, kind="ExternalInput").ap()
    identN = nc.dram_tensor("identN", [128, 128], BF16, kind="ExternalInput").ap()
    ones_in = nc.dram_tensor("ones_in", [128, 512], BF16, kind="ExternalInput").ap()
    triP_in = nc.dram_tensor("triP_in", [128, 256], BF16, kind="ExternalInput").ap()
    triO_in = nc.dram_tensor("triO_in", [128, 256], BF16, kind="ExternalInput").ap()
    m0row = nc.dram_tensor("m0row", [1, 128], BF16, kind="ExternalInput").ap()

    out = nc.dram_tensor("out", [TC, E], F32, kind="ExternalOutput").ap()

    QT = dram("QT", [H, 128, TC], BF16).ap()
    KT = dram("KT", [H, 128, TA], BF16).ap()
    VN = dram("VN", [NTA, 128, E], BF16).ap()
    OT = dram("OT", [H, 128, TC], BF16).ap()
    G = dram("G", [NTC, 128, E], BF16).ap()
    if "XNT" in dbg:
        XNTd = nc.dram_tensor("XNT", [NI, 128, TA], BF16, kind="ExternalOutput").ap()

    prog = Prog()

    with ExitStack() as outer:
        EO = outer.enter_context
        ident = EO(nc.sbuf_tensor("ident", [128, 128], BF16))
        ones_sb = EO(nc.sbuf_tensor("ones_sb", [128, 512], BF16))
        triP01 = EO(nc.sbuf_tensor("triP01", [128, 256], BF16))
        triO01 = EO(nc.sbuf_tensor("triO01", [128, 256], BF16))
        m0_sb = EO(nc.sbuf_tensor("m0_sb", [1, 128], BF16))
        bias_sb = EO(nc.sbuf_tensor("bias_sb", [1, E], BF16))

        SM = lambda name: Sem(nc, outer, name)
        bar = SM("bar")
        const_ld = SM("const_ld")
        bconst_ld = SM("bconst_ld")
        CONST_TOT = 5 * 16

        prog.dma("sync", ident[:], identN, const_ld)
        prog.dma("sync", ones_sb[:], ones_in, const_ld)
        prog.dma("sync", triP01[:], triP_in, const_ld)
        prog.dma("sync", triO01[:], triO_in, const_ld)
        prog.dma("sync", m0_sb[:], m0row, const_ld)
        prog.dma("gpsimd", bias_sb[:], bias1, bconst_ld)

        abc = EO(nc.sbuf_tensor("abc", [128, E], BF16))
        s123 = ExitStack()
        xnt = s123.enter_context(nc.sbuf_tensor("xnt", [128, NI * TA], BF16))
        xnt3 = xnt[:].rearrange("p (i t) -> p i t", i=NI)
        s23 = ExitStack()
        wsl = []
        slab_ld = [SM("slab_ld0"), SM("slab_ld1")]
        slab_free = {}
        slab_tgt = {}

        def load_slab(s):
            if s >= 2:
                sem, cnt = slab_free[s - 2]
                prog.wait("gpsimd", sem, cnt)
            slab_tgt[s] = prog.dma(
                "gpsimd", wsl[s % 2][:],
                wqkvT[:, 512 * s:512 * (s + 1)].rearrange("(i p) e -> p i e", p=128),
                slab_ld[s % 2])

        NCH = 20
        with ExitStack() as p12:
            E1 = p12.enter_context
            wffn = E1(nc.sbuf_tensor("wffn", [128, NI * E], BF16))
            gst = [E1(nc.sbuf_tensor(f"gst{p}", [128, E], BF16)) for p in range(2)]
            sf = [E1(nc.sbuf_tensor(f"sf{p}", [128, 512], F32)) for p in range(2)]
            arow = E1(nc.sbuf_tensor("arow", [1, E], BF16))
            biasbc = E1(nc.sbuf_tensor("biasbc", [128, E], BF16))
            g_ps = [E1(nc.psum_tensor(f"g_ps{p}", [128, 512], F32)) for p in range(2)]
            tp_ps = [E1(nc.psum_tensor(f"tp_ps{p}", [128, 512], F32)) for p in range(2)]
            al_ps = E1(nc.psum_tensor("al_ps", [1, 4 * 512], F32))
            abst = E1(nc.sbuf_tensor("abst", [128, E], BF16))
            EL = E1
            xbuf = [EL(nc.sbuf_tensor(f"xbuf{p}", [128, E], F32)) for p in range(2)]
            xnat = [EL(nc.sbuf_tensor(f"xnat{p}", [128, E], BF16)) for p in range(2)]
            stats = EL(nc.sbuf_tensor("stats", [128, 24], F32))
            mv = [EL(nc.sbuf_tensor(f"mv{p}", [128, 2], F32)) for p in range(2)]
            veps = [EL(nc.sbuf_tensor(f"veps{p}", [128, 1], F32)) for p in range(2)]
            sqv = [EL(nc.sbuf_tensor(f"sqv{p}", [128, 1], F32)) for p in range(2)]
            rstd = [EL(nc.sbuf_tensor(f"rstd{p}", [128, 1], F32)) for p in range(2)]
            nmr = [EL(nc.sbuf_tensor(f"nmr{p}", [128, 1], F32)) for p in range(2)]

            x_ld = [SM(f"x_ld{p}") for p in range(2)]
            v_self = SM("v_self")
            s_self = SM("s_self")
            norm_done = SM("norm_done")
            tp_done = SM("tp_done")
            tp_evV = SM("tp_evV")
            tp_evS = SM("tp_evS")
            wff_ld = [SM(f"wff_ld{r}") for r in range(4)]
            abs_dn = SM("abs_dn")
            bin_dn = SM("bin_dn")
            almm = SM("almm")
            alev = SM("alev")
            abmm = SM("abmm")
            abev = SM("abev")
            rbias = SM("rbias")
            gmm = SM("gmm")
            gsf = SM("gsf")
            silu_dn = SM("silu_dn")
            g_st = [SM("g_st0"), SM("g_st1")]

            wffn3 = wffn[:].rearrange("p (i e) -> p i e", i=NI)
            prog.wait("tensor", const_ld, CONST_TOT)

            for c in range(NCH):
                if c >= 2:
                    prog.wait("sync", norm_done, c - 1)
                prog.dma("sync", xbuf[c % 2][:], xh[128 * c:128 * (c + 1), :],
                         x_ld[c % 2])
            for r in range(4):
                prog.dma("gpsimd", wffn3[:, 4 * r:4 * (r + 1), :],
                         wffnT[512 * r:512 * (r + 1), :].rearrange("(i p) e -> p i e", p=128),
                         wff_ld[r])

            def emit_alpha_tile(i):
                prog.wait("scalar", wff_ld[i // 4], 16)
                if i >= 1:
                    prog.wait("scalar", almm, 4 * i)
                prog.op("scalar", lambda e, i=i: e.activation(
                    abst[:], wffn[:, E * i:E * (i + 1)], AF.Abs), abs_dn)
                prog.wait("vector", abs_dn, i + 1)
                prog.op("vector", lambda e, i=i: e.tensor_scalar(
                    wffn[:, E * i:E * (i + 1)], wffn[:, E * i:E * (i + 1)],
                    0.0, 2.0, ALU.is_ge, ALU.mult), bin_dn)
                prog.wait("tensor", abs_dn, i + 1)
                for og in range(4):
                    prog.op("tensor", lambda e, i=i, og=og: e.matmul(
                        al_ps[0:1, 512 * og:512 * (og + 1)], ones_sb[:, 0:1],
                        abst[:, 512 * og:512 * (og + 1)],
                        start=(i == 0), stop=(i == NI - 1)), almm)

            def emit_alpha_finish():
                prog.wait("scalar", almm, 4 * NI)
                for og in range(4):
                    prog.op("scalar", lambda e, og=og: e.activation(
                        arow[0:1, 512 * og:512 * (og + 1)],
                        al_ps[0:1, 512 * og:512 * (og + 1)], AF.Copy,
                        scale=1.0 / E), alev)
                prog.wait("tensor", alev, 4)
                prog.wait("tensor", bconst_ld, 16)
                for g in range(8):
                    gp = g % 2
                    src = arow if g < 4 else bias_sb
                    dstb = abc if g < 4 else biasbc
                    o = (g % 4) * 512
                    if g >= 2:
                        prog.wait("tensor", abev, g - 1)
                    prog.op("tensor", lambda e, src=src, o=o, gp=gp: e.matmul(
                        g_ps[gp][:], ones_sb[0:1, 0:128], src[0:1, o:o + 512],
                        start=True, stop=True), abmm)
                    prog.wait("vector", abmm, g + 1)
                    prog.op("vector", lambda e, d=dstb, o=o, gp=gp: e.tensor_copy(
                        d[:, o:o + 512], g_ps[gp][:]), abev)

            ffn_mmq = 0
            nmr_tgt = {}

            def emit_ffn_tile(tt):
                nonlocal ffn_mmq
                tp = tt % 2
                if tt == 0:
                    prog.wait("tensor", bin_dn, NI)
                    prog.wait("tensor", abev, 8)
                prog.wait("tensor", tp_evV, 2 * (5 + tt))
                prog.wait("tensor", tp_evS, 2 * (5 + tt))
                for og in range(4):
                    gp = ffn_mmq % 2
                    if ffn_mmq >= 2:
                        prog.wait("tensor", gsf, 2 * ffn_mmq - 3)
                    for i in range(NI):
                        prog.op("tensor", lambda e, tt=tt, og=og, i=i, gp=gp: e.matmul(
                            g_ps[gp][:],
                            xnt[:, TA * i + 512 + 128 * tt: TA * i + 512 + 128 * (tt + 1)],
                            wffn[:, E * i + 512 * og: E * i + 512 * (og + 1)],
                            start=(i == 0), stop=(i == NI - 1)), gmm)
                    prog.wait("vector", gmm, gmm.n)
                    if ffn_mmq >= 2:
                        prog.wait("vector", silu_dn, ffn_mmq - 1)
                    t = prog.op("vector", lambda e, og=og, gp=gp: e.tensor_tensor(
                        sf[gp][:], g_ps[gp][:], abc[:, 512 * og:512 * (og + 1)],
                        ALU.mult), gsf)
                    prog.wait("vector", gsf, t)
                    t = prog.op("vector", lambda e, og=og, gp=gp: e.tensor_tensor(
                        sf[gp][:], sf[gp][:], biasbc[:, 512 * og:512 * (og + 1)],
                        ALU.add), gsf)
                    prog.wait("scalar", gsf, t)
                    if og == 0 and tt >= 2:
                        prog.wait("scalar", g_st[tp], 16 * (tt // 2))
                    prog.op("scalar", lambda e, og=og, tp=tp, gp=gp: e.activation(
                        gst[tp][:, 512 * og:512 * (og + 1)], sf[gp][:], AF.Silu), silu_dn)
                    ffn_mmq += 1
                prog.wait("scalar", silu_dn, silu_dn.n)
                prog.dma("scalar", G[tt], gst[tp][:], g_st[tp])

            def ln_compute(c):
                p = c % 2
                prog.wait("vector", x_ld[c % 2], 16 * (c // 2 + 1))
                if c >= 2:
                    prog.wait("vector", norm_done, c - 1)
                for g4 in range(4):
                    t = prog.op("vector", lambda e, g=g4, p=p: e.bn_stats(
                        stats[:, 6 * g:6 * (g + 1)],
                        xbuf[p][:, 512 * g: 512 * (g + 1)]), v_self)
                prog.wait("vector", v_self, t)
                t = prog.op("vector", lambda e, p=p: e.bn_aggr(mv[p][:], stats[:]), v_self)
                prog.wait("vector", v_self, t)
                t = prog.op("vector", lambda e, p=p: e.tensor_scalar(
                    veps[p][:], mv[p][:, 1:2], 1e-5, None, ALU.add), v_self)
                prog.wait("scalar", v_self, t)
                t = prog.op("scalar", lambda e, p=p: e.activation(
                    sqv[p][:], veps[p][:], AF.Sqrt), s_self)
                prog.wait("vector", s_self, t)
                t = prog.op("vector", lambda e, p=p: e.reciprocal(rstd[p][:], sqv[p][:]), v_self)
                prog.wait("vector", v_self, t)
                tn = prog.op("vector", lambda e, p=p: e.scalar_tensor_tensor(
                    nmr[p][:], mv[p][:, 0:1], -1.0, rstd[p][:], ALU.mult, ALU.mult), v_self)
                nmr_tgt[c] = tn

            def ln_norm(c):
                p = c % 2
                prog.wait("scalar", v_self, nmr_tgt[c])
                if c >= 2:
                    prog.wait("scalar", tp_done, 16 * (c - 1))
                prog.op("scalar", lambda e, p=p: e.activation(
                    xnat[p][:], xbuf[p][:], AF.Identity,
                    bias=nmr[p][:], scale=rstd[p][:]), norm_done)

            def tp_emit(c):
                p = c % 2
                prog.wait("tensor", norm_done, c + 1)
                for g in range(4):
                    gn = 4 * c + g
                    gpar = gn % 2
                    if gn >= 2:
                        prog.wait("tensor", tp_evV if gpar == 0 else tp_evS, gn // 2)
                    for kk in range(4):
                        i = 4 * g + kk
                        prog.op("tensor", lambda e, i=i, kk=kk, p=p, gp=gpar: e.matmul(
                            tp_ps[gp][:, 128 * kk:128 * (kk + 1)],
                            xnat[p][:, 128 * i: 128 * (i + 1)],
                            ident[:], start=True, stop=True), tp_done)
                    tgt = tp_done.n
                    prog.wait("vector" if gpar == 0 else "scalar", tp_done, tgt)
                    dst = xnt3[:, 4 * g:4 * g + 4, 128 * c: 128 * (c + 1)]
                    if gpar == 0:
                        prog.op("vector", lambda e, d=dst, gp=gpar: e.tensor_copy(
                            d, tp_ps[gp][:].rearrange("p (i t) -> p i t", i=4)), tp_evV)
                    else:
                        prog.op("scalar", lambda e, d=dst, gp=gpar: e.copy(
                            d, tp_ps[gp][:].rearrange("p (i t) -> p i t", i=4)), tp_evS)

            ffn_next = 0
            for c in range(NCH + 2):
                for i in range(NI):
                    if i // 3 + 1 == c:
                        emit_alpha_tile(i)
                if c == 7:
                    emit_alpha_finish()
                if 1 <= c <= NCH:
                    ln_norm(c - 1)
                if c >= 2:
                    tp_emit(c - 2)
                if c < NCH:
                    ln_compute(c)
                burst = 0
                while (c >= 8 and ffn_next < NTC and 4 + ffn_next <= c - 2
                       and burst < 2):
                    emit_ffn_tile(ffn_next)
                    ffn_next += 1
                    burst += 1
            for tt in range(ffn_next, NTC):
                emit_ffn_tile(tt)
            if "XNT" in dbg:
                prog.wait("sync", tp_evV, tp_evV.n)
                prog.wait("sync", tp_evS, tp_evS.n)
                dmp = SM("dmp")
                for hf in range(2):
                    prog.dma("sync",
                             XNTd[8 * hf:8 * (hf + 1)].rearrange("i p t -> p i t"),
                             xnt3[:, 8 * hf:8 * (hf + 1), :], dmp)
                prog.wait("sync", dmp, dmp.n)
            gst_tot = [g_st[0].n, g_st[1].n]

        wsl.append(s23.enter_context(nc.sbuf_tensor("wsl0", [128, NI * 512], BF16)))
        wsl.append(s23.enter_context(nc.sbuf_tensor("wsl1", [128, NI * 512], BF16)))
        prog.barrier(bar)
        prog.wait("gpsimd", g_st[0], gst_tot[0])
        prog.wait("gpsimd", g_st[1], gst_tot[1])
        load_slab(0)
        load_slab(1)
        if phases < 3:
            s23.close()
            s123.close()
            finish(prog, nc)
            return nc

        with ExitStack() as p3:
            E3 = p3.enter_context
            qkst = [E3(nc.sbuf_tensor(f"qkst{p}", [128, TA], BF16)) for p in range(2)]
            vstb = [E3(nc.sbuf_tensor(f"vstb{p}", [128, 4 * 512], BF16)) for p in range(2)]
            qk_ps = [E3(nc.psum_tensor(f"qk_ps{p}", [128, 512], F32)) for p in range(2)]

            qkmm = SM("qkmm")
            qkev = SM("qkev")
            qkst_dn = [SM("qkst_dn0"), SM("qkst_dn1")]
            vmm = SM("vmm")
            vev = SM("vev")
            vst_dn = [SM("vst_dn0"), SM("vst_dn1")]

            for s in range(8):
                sp = s % 2
                is_q = s < 4
                ntg = 4 if is_q else 5
                toff = 512 if is_q else 0
                ev_eng = "scalar" if is_q else "vector"
                prog.wait("tensor", slab_ld[sp], slab_tgt[s])
                for el in range(4):
                    h = (4 * s + el) % 16
                    eg = 4 * s + el
                    stp = eg % 2
                    if eg >= 2:
                        prog.wait(ev_eng, qkst_dn[stp], 16 * (eg // 2))
                    for tg in range(ntg):
                        gp = qkmm.n % 2
                        if qkmm.n >= 2:
                            prog.wait("tensor", qkev, qkmm.n - 1)
                        for i in range(NI):
                            prog.op("tensor", lambda e, sp=sp, el=el, tg=tg, i=i, gp=gp, to=toff: e.matmul(
                                qk_ps[gp][:],
                                wsl[sp][:, 512 * i + 128 * el: 512 * i + 128 * (el + 1)],
                                xnt[:, TA * i + to + 512 * tg: TA * i + to + 512 * (tg + 1)],
                                start=(i == 0), stop=(i == NI - 1)),
                                qkmm if i == NI - 1 else None)
                        prog.wait(ev_eng, qkmm, qkmm.n)
                        if is_q:
                            prog.op("scalar", lambda e, tg=tg, stp=stp, gp=gp: e.activation(
                                qkst[stp][:, 512 * tg:512 * (tg + 1)], qk_ps[gp][:],
                                AF.Copy, scale=SCALE), qkev)
                        else:
                            prog.op("vector", lambda e, tg=tg, stp=stp, gp=gp: e.tensor_copy(
                                qkst[stp][:, 512 * tg:512 * (tg + 1)], qk_ps[gp][:]), qkev)
                    prog.wait("sync", qkev, qkev.n)
                    if is_q:
                        prog.dma("sync", QT[h], qkst[stp][:, 0:TC], qkst_dn[stp])
                    else:
                        prog.dma("sync", KT[h], qkst[stp][:, 0:TA], qkst_dn[stp])
                slab_free[s] = (qkmm, qkmm.n)
                if s + 2 < 12:
                    load_slab(s + 2)
            for s in range(8, 12):
                sp = s % 2
                g = s - 8
                prog.wait("tensor", slab_ld[sp], slab_tgt[s])
                for tq in range(5):
                    qg = 5 * g + tq
                    vp = qg % 2
                    if qg >= 2:
                        prog.wait("vector", vst_dn[vp], 16 * (qg // 2))
                    for a in range(4):
                        tau = 4 * tq + a
                        gp = vmm.n % 2
                        if vmm.n >= 2:
                            prog.wait("tensor", vev, vmm.n - 1)
                        for i in range(NI):
                            prog.op("tensor", lambda e, sp=sp, tau=tau, i=i, gp=gp: e.matmul(
                                qk_ps[gp][:],
                                xnt[:, TA * i + 128 * tau: TA * i + 128 * (tau + 1)],
                                wsl[sp][:, 512 * i:512 * (i + 1)],
                                start=(i == 0), stop=(i == NI - 1)),
                                vmm if i == NI - 1 else None)
                        prog.wait("vector", vmm, vmm.n)
                        prog.op("vector", lambda e, a=a, vp=vp, gp=gp: e.tensor_copy(
                            vstb[vp][:, 512 * a:512 * (a + 1)], qk_ps[gp][:]), vev)
                    prog.wait("sync", vev, vev.n)
                    prog.dma("sync",
                             VN[4 * tq:4 * tq + 4, :, 512 * g:512 * (g + 1)].rearrange(
                                 "a p e -> p a e"),
                             vstb[vp][:].rearrange("p (a e) -> p a e", a=4), vst_dn[vp])
                slab_free[s] = (vmm, vmm.n)
                if s + 2 < 12:
                    load_slab(s + 2)
            for sd in qkst_dn + vst_dn:
                prog.wait("sync", sd, sd.n)

        s23.close()
        s123.close()
        prog.barrier(bar)
        if phases < 4:
            finish(prog, nc)
            return nc

        with ExitStack() as p4:
            E4 = p4.enter_context
            qtb = [E4(nc.sbuf_tensor(f"qtb{p}", [128, H * 512], BF16)) for p in range(2)]
            ktw = E4(nc.sbuf_tensor("ktw", [128, H * 3 * 512], BF16))
            vnw = E4(nc.sbuf_tensor("vnw", [128, 3 * 4 * E], BF16))
            pts = [E4(nc.sbuf_tensor(f"pts{p}", [128, 8 * 1024], BF16)) for p in range(2)]
            rlb = [E4(nc.sbuf_tensor(f"rlb{p}", [128, 1024], F32)) for p in range(2)]
            rltmp = E4(nc.sbuf_tensor("rltmp", [128, 1024], F32))
            otst = E4(nc.sbuf_tensor("otst", [128, H * 512], BF16))
            otraw = [E4(nc.sbuf_tensor(f"otraw{p}", [128, 1024], BF16)) for p in range(2)]
            s4 = [E4(nc.psum_tensor(f"s4_{p}", [128, 1024], F32)) for p in range(2)]
            ob2 = E4(nc.psum_tensor("ob2", [128, 1024], F32))
            lb2 = E4(nc.psum_tensor("lb2", [128, 1024], F32))

            kv_ld = [SM("kv_ld0"), SM("kv_ld1"), SM("kv_ld2")]
            vn_ld = [SM("vn_ld0"), SM("vn_ld1"), SM("vn_ld2")]
            qt_ld = [SM("qt_ld0"), SM("qt_ld1")]
            smm = SM("smm")
            expd = SM("expd")
            mskd = SM("mskd")
            avl = SM("avl")
            lnd = SM("lnd")
            rld = SM("rld")
            oraw = SM("oraw")
            oev = SM("oev")
            ot_st = SM("ot_st")

            for sd in qkst_dn + vst_dn:
                prog.wait("sync", sd, sd.n)

            def load_kv(gc):
                sl = gc % 3
                prog.dma("sync",
                         ktw[:].rearrange("p (h r t) -> p h r t", h=H, r=3)[:, :, sl, :],
                         KT[:, :, 512 * gc:512 * (gc + 1)].rearrange("h p t -> p h t"),
                         kv_ld[sl])
                prog.dma("gpsimd",
                         vnw[:].rearrange("p (r a e) -> p r a e", r=3, a=4)[:, sl],
                         VN[4 * gc:4 * (gc + 1)].rearrange("a p e -> p a e"),
                         vn_ld[sl])
                return kv_ld[sl].n

            def load_qt(c):
                prog.dma("scalar", qtb[c % 2][:].rearrange("p (h t) -> p h t", h=H),
                         QT[:, :, 512 * c:512 * (c + 1)].rearrange("h p t -> p h t"),
                         qt_ld[c % 2])
                return qt_ld[c % 2].n

            kv_rdy = {0: load_kv(0), 1: load_kv(1)}
            qt_rdy = {0: load_qt(0)}

            for cc in range(4):
                gc = cc + 1
                if cc < 3:
                    if cc >= 1:
                        prog.wait("sync", avl, cc * 8)
                        prog.wait("gpsimd", avl, cc * 8)
                        prog.wait("scalar", avl, cc * 8)
                    kv_rdy[gc + 1] = load_kv(gc + 1)
                    qt_rdy[cc + 1] = load_qt(cc + 1)
                prog.wait("tensor", kv_ld[(gc - 1) % 3], kv_rdy[gc - 1])
                prog.wait("tensor", kv_ld[gc % 3], kv_rdy[gc])
                prog.wait("tensor", vn_ld[(gc - 1) % 3], kv_rdy[gc - 1])
                prog.wait("tensor", vn_ld[gc % 3], kv_rdy[gc])
                prog.wait("tensor", qt_ld[cc % 2], qt_rdy[cc])
                for h2 in range(H // 2):
                    gp2 = cc * 8 + h2
                    pp = h2 % 2
                    if gp2 >= 2:
                        prog.wait("scalar", avl, gp2 - 1)
                    jobs = [("own", jj, 128 * jj, 512, gc % 3, jj) for jj in range(4)] + \
                           [("prev", jj, 0, 128 * (jj + 1), (gc - 1) % 3, jj) for jj in range(4)]

                    def emit_sjob(idx, h2=h2, gp2=gp2, cc=cc, jobs=jobs, pp=pp):
                        gq = gp2 * 8 + idx
                        kind, jj, qlo, qhi, sl, jt = jobs[idx]
                        mend = (kind == "prev" and cc == 0)
                        if gq >= 2:
                            prog.wait("tensor", expd, gq - 1)
                        for a in range(2):
                            h = 2 * h2 + a
                            last = (a == 1) and not mend
                            prog.op("tensor", lambda e, h=h, jj=jj, qlo=qlo, qhi=qhi, sl=sl,
                                    gq=gq, a=a, cp=cc % 2, mend=mend: e.matmul(
                                s4[gq % 2][:, 512 * a + qlo:512 * a + qhi],
                                ktw[:, 1536 * h + 512 * sl + 128 * jj: 1536 * h + 512 * sl + 128 * (jj + 1)],
                                qtb[cp][:, 512 * h + qlo: 512 * h + qhi],
                                start=True, stop=not mend),
                                smm if last else None)
                            if mend:
                                prog.op("tensor", lambda e, qlo=qlo, qhi=qhi, gq=gq, a=a: e.matmul(
                                    s4[gq % 2][:, 512 * a + qlo:512 * a + qhi], m0_sb[0:1, :],
                                    ones_sb[0:1, 0:qhi - qlo],
                                    start=False, stop=True),
                                    smm if a == 1 else None)
                        prog.wait("scalar", smm, smm.n)
                        prog.op("scalar", lambda e, idx=idx, qlo=qlo, qhi=qhi, gq=gq, pp=pp: e.activation(
                            pts[pp][:].rearrange("p (s a w) -> p s a w", s=8, a=2)[:, idx, :, qlo:qhi],
                            s4[gq % 2][:].rearrange("p (a w) -> p a w", a=2)[:, :, qlo:qhi],
                            AF.Exp), expd)
                        prog.wait("vector", expd, expd.n)
                        tri = triO01 if kind == "own" else triP01
                        base = 8 * idx + jj
                        dst = pts[pp][:].rearrange("p (x w) -> p x w", w=128)[
                            :, base:base + 5:4, :]
                        prog.op("vector", lambda e, d=dst, m=tri: e.tensor_tensor(
                            d, d, m[:].rearrange("p (a w) -> p a w", a=2), ALU.mult), mskd)

                    def emit_avl(idx, h2=h2, gp2=gp2, jobs=jobs, pp=pp):
                        kind, jj, qlo, qhi, sl, jt = jobs[idx]
                        prog.wait("tensor", mskd, gp2 * 8 + idx + 1)
                        for a in range(2):
                            h = 2 * h2 + a
                            prog.op("tensor", lambda e, h=h, sl=sl, jt=jt, idx=idx, a=a,
                                    qlo=qlo, qhi=qhi, pp=pp: e.matmul(
                                ob2[:, 512 * a + qlo:512 * a + qhi],
                                vnw[:, 8192 * sl + 2048 * jt + 128 * h: 8192 * sl + 2048 * jt + 128 * (h + 1)],
                                pts[pp][:, 1024 * idx + 512 * a + qlo: 1024 * idx + 512 * a + qhi],
                                start=(idx == 0), stop=(idx == 7)))
                        for a in range(2):
                            prog.op("tensor", lambda e, idx=idx, a=a, qlo=qlo, qhi=qhi, pp=pp: e.matmul(
                                lb2[:, 512 * a + qlo:512 * a + qhi], ones_sb[:, 0:128],
                                pts[pp][:, 1024 * idx + 512 * a + qlo: 1024 * idx + 512 * a + qhi],
                                start=(idx == 0), stop=(idx == 7)),
                                avl if idx == 7 and a == 1 else None)

                    emit_sjob(0)
                    emit_sjob(1)
                    for idx in range(8):
                        if idx + 2 < 8:
                            emit_sjob(idx + 2)
                        if idx == 0 and gp2 >= 1:
                            prog.wait("tensor", oraw, gp2)
                            prog.wait("tensor", lnd, gp2)
                        emit_avl(idx)
                    prog.wait("vector", avl, avl.n)
                    prog.op("vector", lambda e, pp=pp: e.tensor_copy(
                        otraw[pp][:], ob2[:]), oraw)
                    prog.wait("scalar", avl, avl.n)
                    prog.op("scalar", lambda e: e.activation(
                        rltmp[:], lb2[:], AF.Ln), lnd)
                    if gp2 >= 2:
                        prog.wait("scalar", oev, gp2 - 1)
                    t = prog.op("scalar", lambda e, pp=pp: e.activation(
                        rlb[pp][:], rltmp[:], AF.Exp, scale=-1.0), rld)
                    prog.wait("vector", rld, t)
                    if cc >= 1 and h2 == 0:
                        prog.wait("vector", ot_st, 16 * cc)
                    prog.op("vector", lambda e, h2=h2, pp=pp: e.tensor_tensor(
                        otst[:, 1024 * h2:1024 * (h2 + 1)], otraw[pp][:], rlb[pp][:],
                        ALU.mult), oev)
                prog.wait("scalar", oev, oev.n)
                prog.dma("scalar", OT[:, :, 512 * cc:512 * (cc + 1)].rearrange("h p t -> p h t"),
                         otst[:].rearrange("p (h t) -> p h t", h=H), ot_st)
            prog.wait("scalar", ot_st, ot_st.n)

        prog.barrier(bar)
        if phases < 5:
            finish(prog, nc)
            return nc

        with ExitStack() as p5:
            E5 = p5.enter_context
            wout_sb = E5(nc.sbuf_tensor("wout_sb", [128, NI * E], BF16))
            otc = [E5(nc.sbuf_tensor(f"otc{p}", [128, H * 128], BF16)) for p in range(2)]
            gt = [E5(nc.sbuf_tensor(f"gt{p}", [128, E], BF16)) for p in range(2)]
            xt = [E5(nc.sbuf_tensor(f"xt{p}", [128, E], F32)) for p in range(2)]
            sfin = [E5(nc.sbuf_tensor(f"sfin{p}", [128, 512], F32)) for p in range(2)]
            obf = [E5(nc.sbuf_tensor(f"obf{p}", [128, E], F32)) for p in range(2)]
            s5_ps = [E5(nc.psum_tensor(f"s5_ps{p}", [128, 512], F32)) for p in range(2)]

            w5_ld = [SM(f"w5_ld{s}") for s in range(4)]
            t5_ld = [SM("t5_ld0"), SM("t5_ld1")]
            x5_ld = [SM("x5_ld0"), SM("x5_ld1")]
            s5mm = SM("s5mm")
            findn = SM("findn")
            outst = [SM("outst0"), SM("outst1")]
            v5_self = SM("v5_self")

            for sd in g_st:
                prog.wait("sync", sd, sd.n)

            for s in range(4):
                prog.dma("gpsimd",
                         wout_sb[:].rearrange("p (i e) -> p i e", i=NI)[:, :, 512 * s:512 * (s + 1)],
                         woutT[:, 512 * s:512 * (s + 1)].rearrange("(i p) e -> p i e", p=128),
                         w5_ld[s])

            def load_tt(tt):
                tp = tt % 2
                prog.wait("sync", ot_st, 16 * (tt // 4 + 1))
                prog.dma("sync", otc[tp][:].rearrange("p (h t) -> p h t", h=H),
                         OT[:, :, 128 * tt:128 * (tt + 1)].rearrange("h p t -> p h t"),
                         t5_ld[tp])
                prog.dma("sync", gt[tp][:], G[tt], t5_ld[tp])
                if tt >= 2:
                    prog.wait("scalar", outst[tp], 16 * (tt // 2))
                prog.dma("scalar", xt[tp][:],
                         xh[512 + 128 * tt: 512 + 128 * (tt + 1), :], x5_ld[tp])
                return t5_ld[tp].n

            rdy = {0: load_tt(0)}
            mmq = 0
            for tt in range(NTC):
                tp = tt % 2
                if tt < NTC - 1:
                    if tt >= 1:
                        prog.wait("sync", outst[(tt + 1) % 2], 16 * ((tt + 1) // 2))
                    rdy[tt + 1] = load_tt(tt + 1)
                prog.wait("tensor", t5_ld[tp], rdy[tt])
                prog.wait("tensor", x5_ld[tp], 16 * (tt // 2 + 1))
                if tt >= 2:
                    prog.wait("vector", outst[tp], 16 * (tt // 2))
                for og in range(4):
                    gp = mmq % 2
                    if tt == 0:
                        prog.wait("tensor", w5_ld[og], 16)
                    if mmq >= 2:
                        prog.wait("tensor", findn, mmq - 1)
                    for hh in range(H):
                        prog.op("tensor", lambda e, tp=tp, og=og, hh=hh, gp=gp: e.matmul(
                            s5_ps[gp][:],
                            otc[tp][:, 128 * hh:128 * (hh + 1)],
                            wout_sb[:, E * hh + 512 * og: E * hh + 512 * (og + 1)],
                            start=(hh == 0), stop=(hh == H - 1)),
                            s5mm if hh == H - 1 else None)
                    mmq += 1
                    prog.wait("vector", s5mm, mmq)
                    t = prog.op("vector", lambda e, tp=tp, og=og, gp=gp: e.tensor_tensor(
                        sfin[gp][:], s5_ps[gp][:], gt[tp][:, 512 * og:512 * (og + 1)],
                        ALU.mult), v5_self)
                    prog.wait("vector", v5_self, t)
                    prog.op("vector", lambda e, tp=tp, og=og, gp=gp: e.tensor_tensor(
                        obf[tp][:, 512 * og:512 * (og + 1)], sfin[gp][:],
                        xt[tp][:, 512 * og:512 * (og + 1)], ALU.add), findn)
                prog.wait("gpsimd", findn, findn.n)
                prog.dma("gpsimd", out[128 * tt:128 * (tt + 1), :], obf[tp][:], outst[tp])
            prog.wait("sync", outst[0], outst[0].n)
            prog.wait("sync", outst[1], outst[1].n)

        finish(prog, nc)
    return nc


def finish(prog, nc):
    with nc.Block() as block:
        for name in ENGS:
            deco = getattr(block, name)

            def runner(eng, name=name):
                for fn in prog.streams[name]:
                    fn(eng)

            deco(runner)



def _consts():
    ident = np.eye(128, dtype=ml_dtypes.bfloat16)
    ones = np.ones((128, 512), dtype=ml_dtypes.bfloat16)
    p = np.arange(128)[:, None]
    f = np.arange(128)[None, :]
    triP = np.where(p >= f, 1.0, 0.0).astype(ml_dtypes.bfloat16)
    triO = np.where(p <= f, 1.0, 0.0).astype(ml_dtypes.bfloat16)
    triP = np.ascontiguousarray(np.concatenate([triP, triP], axis=1))
    triO = np.ascontiguousarray(np.concatenate([triO, triO], axis=1))
    return ident, ones, triP, triO


def make_in_maps(x, wqkv, wout, wffn, bias):
    ident, ones, triP, triO = _consts()
    wqkvT = np.ascontiguousarray(np.asarray(wqkv, dtype=np.float32).T)
    woutT = np.ascontiguousarray(np.asarray(wout, dtype=np.float32).T)
    wffnT = np.ascontiguousarray(np.asarray(wffn, dtype=np.float32).T)
    bias1 = np.ascontiguousarray(np.asarray(bias, dtype=np.float32).reshape(1, E))
    in_maps = []
    for core in range(8):
        b, j = core // 4, core % 4
        lo = 2048 * j
        if j == 0:
            halo = np.zeros((C, E), dtype=np.float32)
            m0 = np.full((1, 128), NEG, dtype=ml_dtypes.bfloat16)
        else:
            halo = np.asarray(x[b, lo - C:lo], dtype=np.float32)
            m0 = np.zeros((1, 128), dtype=ml_dtypes.bfloat16)
        xhc = np.ascontiguousarray(
            np.concatenate([halo, np.asarray(x[b, lo:lo + TC], dtype=np.float32)], 0))
        in_maps.append(dict(
            xh=xhc, wqkvT=wqkvT, woutT=woutT, wffnT=wffnT, bias1=bias1,
            identN=ident, ones_in=ones, triP_in=triP, triO_in=triO, m0row=m0))
    return in_maps


def kernel(L0_attn_x, L0_attn_w_qkv, L0_attn_w_out, L0_ffn_weight, L0_ffn_bias):
    x = np.asarray(L0_attn_x)
    in_maps = make_in_maps(x, L0_attn_w_qkv, L0_attn_w_out, L0_ffn_weight, L0_ffn_bias)
    nc = build()
    res = run_bass_kernel_spmd(nc, in_maps, core_ids=list(range(8)))
    outa = np.empty((2, 8192, 2048), dtype=np.float32)
    for core in range(8):
        b, j = core // 4, core % 4
        outa[b, 2048 * j:2048 * (j + 1)] = res.results[core]["out"]
    return outa


# version 16 (final)
# speedup vs baseline: 1.0452x; 1.0014x over previous
import math
import numpy as np
import ml_dtypes
from contextlib import ExitStack

import concourse.bass as bass
import concourse.mybir as mybir
from concourse.bass_utils import run_bass_kernel_spmd

F32 = mybir.dt.float32
BF16 = mybir.dt.bfloat16
AF = mybir.ActivationFunctionType
ALU = mybir.AluOpType

NEG = -1.0e30
E = 2048
NI = 16
H = 16
C = 512
TC = 2048
TA = 2560
NTC = 16
NTA = 20
SCALE = 1.0 / math.sqrt(128.0)
ENGS = ("sync", "tensor", "vector", "scalar", "gpsimd")


class Sem:
    def __init__(self, nc, stack, name):
        self.h = stack.enter_context(nc.semaphore(name))
        self.n = 0


class Prog:

    def __init__(self):
        self.streams = {e: [] for e in ENGS}

    def emit(self, eng, fn):
        self.streams[eng].append(fn)

    def wait(self, eng, sem, tgt):
        if tgt > 0:
            self.emit(eng, lambda e, s=sem.h, t=tgt: e.wait_ge(s, t))

    def op(self, eng, builder, sem=None, k=1):
        if sem is None:
            self.emit(eng, builder)
            return None
        sem.n += k
        self.emit(eng, lambda e, b=builder, s=sem.h, kk=k: b(e).then_inc(s, kk))
        return sem.n

    def dma(self, eng, out_ap, in_ap, sem):
        sem.n += 16
        self.emit(eng, lambda e, o=out_ap, i=in_ap, s=sem.h: e.dma_start(o, i).then_inc(s, 16))
        return sem.n

    def barrier(self, bar):
        for e in ENGS:
            self.emit(e, lambda en, s=bar.h: en.nop().then_inc(s, 1))
        bar.n += len(ENGS)
        tgt = bar.n
        for e in ENGS:
            self.emit(e, lambda en, s=bar.h, t=tgt: en.wait_ge(s, t))


def build(dbg=frozenset(), phases=5):
    nc = bass.Bass("TRN2", target_bir_lowering=False)

    def dram(name, shape, dtype):
        kind = "ExternalOutput" if name in dbg else "Internal"
        return nc.dram_tensor(name, shape, dtype, kind=kind)

    xh = nc.dram_tensor("xh", [TA, E], F32, kind="ExternalInput").ap()
    wqkvT = nc.dram_tensor("wqkvT", [E, 3 * E], F32, kind="ExternalInput").ap()
    woutT = nc.dram_tensor("woutT", [E, E], F32, kind="ExternalInput").ap()
    wffnT = nc.dram_tensor("wffnT", [E, E], F32, kind="ExternalInput").ap()
    bias1 = nc.dram_tensor("bias1", [1, E], F32, kind="ExternalInput").ap()
    identN = nc.dram_tensor("identN", [128, 128], BF16, kind="ExternalInput").ap()
    ones_in = nc.dram_tensor("ones_in", [128, 512], BF16, kind="ExternalInput").ap()
    triP_in = nc.dram_tensor("triP_in", [128, 256], BF16, kind="ExternalInput").ap()
    triO_in = nc.dram_tensor("triO_in", [128, 256], BF16, kind="ExternalInput").ap()
    m0row = nc.dram_tensor("m0row", [1, 128], BF16, kind="ExternalInput").ap()

    out = nc.dram_tensor("out", [TC, E], F32, kind="ExternalOutput").ap()

    QT = dram("QT", [H, 128, TC], BF16).ap()
    KT = dram("KT", [H, 128, TA], BF16).ap()
    VN = dram("VN", [NTA, 128, E], BF16).ap()
    OT = dram("OT", [H, 128, TC], BF16).ap()
    G = dram("G", [NTC, 128, E], BF16).ap()
    if "XNT" in dbg:
        XNTd = nc.dram_tensor("XNT", [NI, 128, TA], BF16, kind="ExternalOutput").ap()

    prog = Prog()

    with ExitStack() as outer:
        EO = outer.enter_context
        ident = EO(nc.sbuf_tensor("ident", [128, 128], BF16))
        ones_sb = EO(nc.sbuf_tensor("ones_sb", [128, 512], BF16))
        triP01 = EO(nc.sbuf_tensor("triP01", [128, 256], BF16))
        triO01 = EO(nc.sbuf_tensor("triO01", [128, 256], BF16))
        m0_sb = EO(nc.sbuf_tensor("m0_sb", [1, 128], BF16))
        bias_sb = EO(nc.sbuf_tensor("bias_sb", [1, E], BF16))

        SM = lambda name: Sem(nc, outer, name)
        bar = SM("bar")
        const_ld = SM("const_ld")
        bconst_ld = SM("bconst_ld")
        CONST_TOT = 5 * 16

        prog.dma("sync", ident[:], identN, const_ld)
        prog.dma("sync", ones_sb[:], ones_in, const_ld)
        prog.dma("sync", triP01[:], triP_in, const_ld)
        prog.dma("sync", triO01[:], triO_in, const_ld)
        prog.dma("sync", m0_sb[:], m0row, const_ld)
        prog.dma("gpsimd", bias_sb[:], bias1, bconst_ld)

        abc = EO(nc.sbuf_tensor("abc", [128, E], BF16))
        s123 = ExitStack()
        xnt = s123.enter_context(nc.sbuf_tensor("xnt", [128, NI * TA], BF16))
        xnt3 = xnt[:].rearrange("p (i t) -> p i t", i=NI)
        s23 = ExitStack()
        wsl = []
        slab_ld = [SM("slab_ld0"), SM("slab_ld1")]
        slab_free = {}
        slab_tgt = {}

        def load_slab(s):
            if s >= 2:
                sem, cnt = slab_free[s - 2]
                prog.wait("gpsimd", sem, cnt)
            slab_tgt[s] = prog.dma(
                "gpsimd", wsl[s % 2][:],
                wqkvT[:, 512 * s:512 * (s + 1)].rearrange("(i p) e -> p i e", p=128),
                slab_ld[s % 2])

        NCH = 20
        with ExitStack() as p12:
            E1 = p12.enter_context
            wffn = E1(nc.sbuf_tensor("wffn", [128, NI * E], BF16))
            gst = [E1(nc.sbuf_tensor(f"gst{p}", [128, E], BF16)) for p in range(2)]
            sf = [E1(nc.sbuf_tensor(f"sf{p}", [128, 512], F32)) for p in range(2)]
            arow = E1(nc.sbuf_tensor("arow", [1, E], BF16))
            biasbc = E1(nc.sbuf_tensor("biasbc", [128, E], BF16))
            g_ps = [E1(nc.psum_tensor(f"g_ps{p}", [128, 512], F32)) for p in range(2)]
            tp_ps = [E1(nc.psum_tensor(f"tp_ps{p}", [128, 512], F32)) for p in range(2)]
            al_ps = E1(nc.psum_tensor("al_ps", [1, 4 * 512], F32))
            abst = E1(nc.sbuf_tensor("abst", [128, E], BF16))
            EL = E1
            xbuf = [EL(nc.sbuf_tensor(f"xbuf{p}", [128, E], F32)) for p in range(2)]
            xnat = [EL(nc.sbuf_tensor(f"xnat{p}", [128, E], BF16)) for p in range(2)]
            stats = EL(nc.sbuf_tensor("stats", [128, 24], F32))
            mv = [EL(nc.sbuf_tensor(f"mv{p}", [128, 2], F32)) for p in range(2)]
            veps = [EL(nc.sbuf_tensor(f"veps{p}", [128, 1], F32)) for p in range(2)]
            sqv = [EL(nc.sbuf_tensor(f"sqv{p}", [128, 1], F32)) for p in range(2)]
            rstd = [EL(nc.sbuf_tensor(f"rstd{p}", [128, 1], F32)) for p in range(2)]
            nmr = [EL(nc.sbuf_tensor(f"nmr{p}", [128, 1], F32)) for p in range(2)]

            x_ld = [SM(f"x_ld{p}") for p in range(2)]
            v_self = SM("v_self")
            s_self = SM("s_self")
            norm_done = SM("norm_done")
            tp_done = SM("tp_done")
            tp_evV = SM("tp_evV")
            tp_evS = SM("tp_evS")
            wff_ld = [SM(f"wff_ld{r}") for r in range(4)]
            abs_dn = SM("abs_dn")
            bin_dn = SM("bin_dn")
            almm = SM("almm")
            alev = SM("alev")
            abmm = SM("abmm")
            abev = SM("abev")
            rbias = SM("rbias")
            gmm = SM("gmm")
            gsf = SM("gsf")
            silu_dn = SM("silu_dn")
            g_st = [SM("g_st0"), SM("g_st1")]

            wffn3 = wffn[:].rearrange("p (i e) -> p i e", i=NI)
            prog.wait("tensor", const_ld, CONST_TOT)

            for c in range(NCH):
                if c >= 2:
                    prog.wait("sync", norm_done, c - 1)
                prog.dma("sync", xbuf[c % 2][:], xh[128 * c:128 * (c + 1), :],
                         x_ld[c % 2])
            for r in range(4):
                prog.dma("gpsimd", wffn3[:, 4 * r:4 * (r + 1), :],
                         wffnT[512 * r:512 * (r + 1), :].rearrange("(i p) e -> p i e", p=128),
                         wff_ld[r])

            def emit_alpha_tile(i):
                prog.wait("scalar", wff_ld[i // 4], 16)
                if i >= 1:
                    prog.wait("scalar", almm, 4 * i)
                prog.op("scalar", lambda e, i=i: e.activation(
                    abst[:], wffn[:, E * i:E * (i + 1)], AF.Abs), abs_dn)
                prog.wait("vector", abs_dn, i + 1)
                prog.op("vector", lambda e, i=i: e.tensor_scalar(
                    wffn[:, E * i:E * (i + 1)], wffn[:, E * i:E * (i + 1)],
                    0.0, 2.0, ALU.is_ge, ALU.mult), bin_dn)
                prog.wait("tensor", abs_dn, i + 1)
                for og in range(4):
                    prog.op("tensor", lambda e, i=i, og=og: e.matmul(
                        al_ps[0:1, 512 * og:512 * (og + 1)], ones_sb[:, 0:1],
                        abst[:, 512 * og:512 * (og + 1)],
                        start=(i == 0), stop=(i == NI - 1)), almm)

            def emit_alpha_finish():
                prog.wait("scalar", almm, 4 * NI)
                for og in range(4):
                    prog.op("scalar", lambda e, og=og: e.activation(
                        arow[0:1, 512 * og:512 * (og + 1)],
                        al_ps[0:1, 512 * og:512 * (og + 1)], AF.Copy,
                        scale=1.0 / E), alev)
                prog.wait("tensor", alev, 4)
                prog.wait("tensor", bconst_ld, 16)
                for g in range(8):
                    gp = g % 2
                    src = arow if g < 4 else bias_sb
                    dstb = abc if g < 4 else biasbc
                    o = (g % 4) * 512
                    if g >= 2:
                        prog.wait("tensor", abev, g - 1)
                    prog.op("tensor", lambda e, src=src, o=o, gp=gp: e.matmul(
                        g_ps[gp][:], ones_sb[0:1, 0:128], src[0:1, o:o + 512],
                        start=True, stop=True), abmm)
                    prog.wait("vector", abmm, g + 1)
                    prog.op("vector", lambda e, d=dstb, o=o, gp=gp: e.tensor_copy(
                        d[:, o:o + 512], g_ps[gp][:]), abev)

            ffn_mmq = 0
            nmr_tgt = {}

            def emit_ffn_tile(tt):
                nonlocal ffn_mmq
                tp = tt % 2
                if tt == 0:
                    prog.wait("tensor", bin_dn, NI)
                    prog.wait("tensor", abev, 8)
                prog.wait("tensor", tp_evV, 2 * (5 + tt))
                prog.wait("tensor", tp_evS, 2 * (5 + tt))
                for og in range(4):
                    gp = ffn_mmq % 2
                    if ffn_mmq >= 2:
                        prog.wait("tensor", gsf, 2 * ffn_mmq - 3)
                    for i in range(NI):
                        prog.op("tensor", lambda e, tt=tt, og=og, i=i, gp=gp: e.matmul(
                            g_ps[gp][:],
                            xnt[:, TA * i + 512 + 128 * tt: TA * i + 512 + 128 * (tt + 1)],
                            wffn[:, E * i + 512 * og: E * i + 512 * (og + 1)],
                            start=(i == 0), stop=(i == NI - 1)), gmm)
                    prog.wait("vector", gmm, gmm.n)
                    if ffn_mmq >= 2:
                        prog.wait("vector", silu_dn, ffn_mmq - 1)
                    t = prog.op("vector", lambda e, og=og, gp=gp: e.tensor_tensor(
                        sf[gp][:], g_ps[gp][:], abc[:, 512 * og:512 * (og + 1)],
                        ALU.mult), gsf)
                    prog.wait("vector", gsf, t)
                    t = prog.op("vector", lambda e, og=og, gp=gp: e.tensor_tensor(
                        sf[gp][:], sf[gp][:], biasbc[:, 512 * og:512 * (og + 1)],
                        ALU.add), gsf)
                    prog.wait("scalar", gsf, t)
                    if og == 0 and tt >= 2:
                        prog.wait("scalar", g_st[tp], 16 * (tt // 2))
                    prog.op("scalar", lambda e, og=og, tp=tp, gp=gp: e.activation(
                        gst[tp][:, 512 * og:512 * (og + 1)], sf[gp][:], AF.Silu), silu_dn)
                    ffn_mmq += 1
                prog.wait("scalar", silu_dn, silu_dn.n)
                prog.dma("scalar", G[tt], gst[tp][:], g_st[tp])

            def ln_compute(c):
                p = c % 2
                prog.wait("vector", x_ld[c % 2], 16 * (c // 2 + 1))
                if c >= 2:
                    prog.wait("vector", norm_done, c - 1)
                for g4 in range(4):
                    t = prog.op("vector", lambda e, g=g4, p=p: e.bn_stats(
                        stats[:, 6 * g:6 * (g + 1)],
                        xbuf[p][:, 512 * g: 512 * (g + 1)]), v_self)
                prog.wait("vector", v_self, t)
                t = prog.op("vector", lambda e, p=p: e.bn_aggr(mv[p][:], stats[:]), v_self)
                prog.wait("vector", v_self, t)
                t = prog.op("vector", lambda e, p=p: e.tensor_scalar(
                    veps[p][:], mv[p][:, 1:2], 1e-5, None, ALU.add), v_self)
                prog.wait("scalar", v_self, t)
                t = prog.op("scalar", lambda e, p=p: e.activation(
                    sqv[p][:], veps[p][:], AF.Sqrt), s_self)
                prog.wait("vector", s_self, t)
                t = prog.op("vector", lambda e, p=p: e.reciprocal(rstd[p][:], sqv[p][:]), v_self)
                prog.wait("vector", v_self, t)
                tn = prog.op("vector", lambda e, p=p: e.scalar_tensor_tensor(
                    nmr[p][:], mv[p][:, 0:1], -1.0, rstd[p][:], ALU.mult, ALU.mult), v_self)
                nmr_tgt[c] = tn

            def ln_norm(c):
                p = c % 2
                prog.wait("scalar", v_self, nmr_tgt[c])
                if c >= 2:
                    prog.wait("scalar", tp_done, 16 * (c - 1))
                prog.op("scalar", lambda e, p=p: e.activation(
                    xnat[p][:], xbuf[p][:], AF.Identity,
                    bias=nmr[p][:], scale=rstd[p][:]), norm_done)

            def tp_emit(c):
                p = c % 2
                prog.wait("tensor", norm_done, c + 1)
                for g in range(4):
                    gn = 4 * c + g
                    gpar = gn % 2
                    if gn >= 2:
                        prog.wait("tensor", tp_evV if gpar == 0 else tp_evS, gn // 2)
                    for kk in range(4):
                        i = 4 * g + kk
                        prog.op("tensor", lambda e, i=i, kk=kk, p=p, gp=gpar: e.matmul(
                            tp_ps[gp][:, 128 * kk:128 * (kk + 1)],
                            xnat[p][:, 128 * i: 128 * (i + 1)],
                            ident[:], start=True, stop=True), tp_done)
                    tgt = tp_done.n
                    prog.wait("vector" if gpar == 0 else "scalar", tp_done, tgt)
                    dst = xnt3[:, 4 * g:4 * g + 4, 128 * c: 128 * (c + 1)]
                    if gpar == 0:
                        prog.op("vector", lambda e, d=dst, gp=gpar: e.tensor_copy(
                            d, tp_ps[gp][:].rearrange("p (i t) -> p i t", i=4)), tp_evV)
                    else:
                        prog.op("scalar", lambda e, d=dst, gp=gpar: e.copy(
                            d, tp_ps[gp][:].rearrange("p (i t) -> p i t", i=4)), tp_evS)

            ffn_next = 0
            for c in range(NCH + 2):
                for i in range(NI):
                    if i // 3 + 1 == c:
                        emit_alpha_tile(i)
                if c == 7:
                    emit_alpha_finish()
                if 1 <= c <= NCH:
                    ln_norm(c - 1)
                if c >= 2:
                    tp_emit(c - 2)
                if c < NCH:
                    ln_compute(c)
                burst = 0
                while (c >= 8 and ffn_next < NTC and 4 + ffn_next <= c - 2
                       and burst < 2):
                    emit_ffn_tile(ffn_next)
                    ffn_next += 1
                    burst += 1
            for tt in range(ffn_next, NTC):
                emit_ffn_tile(tt)
            if "XNT" in dbg:
                prog.wait("sync", tp_evV, tp_evV.n)
                prog.wait("sync", tp_evS, tp_evS.n)
                dmp = SM("dmp")
                for hf in range(2):
                    prog.dma("sync",
                             XNTd[8 * hf:8 * (hf + 1)].rearrange("i p t -> p i t"),
                             xnt3[:, 8 * hf:8 * (hf + 1), :], dmp)
                prog.wait("sync", dmp, dmp.n)
            gst_tot = [g_st[0].n, g_st[1].n]

        wsl.append(s23.enter_context(nc.sbuf_tensor("wsl0", [128, NI * 512], BF16)))
        wsl.append(s23.enter_context(nc.sbuf_tensor("wsl1", [128, NI * 512], BF16)))
        prog.barrier(bar)
        prog.wait("gpsimd", g_st[0], gst_tot[0])
        prog.wait("gpsimd", g_st[1], gst_tot[1])
        load_slab(0)
        load_slab(1)
        if phases < 3:
            s23.close()
            s123.close()
            finish(prog, nc)
            return nc

        with ExitStack() as p3:
            E3 = p3.enter_context
            qkst = [E3(nc.sbuf_tensor(f"qkst{p}", [128, TA], BF16)) for p in range(2)]
            vstb = [E3(nc.sbuf_tensor(f"vstb{p}", [128, 4 * 512], BF16)) for p in range(2)]
            qk_ps = [E3(nc.psum_tensor(f"qk_ps{p}", [128, 512], F32)) for p in range(2)]

            qkmm = SM("qkmm")
            qkev = SM("qkev")
            qkst_dn = [SM("qkst_dn0"), SM("qkst_dn1")]
            vmm = SM("vmm")
            vev = SM("vev")
            vst_dn = [SM("vst_dn0"), SM("vst_dn1")]

            for s in range(8):
                sp = s % 2
                is_q = s < 4
                ntg = 4 if is_q else 5
                toff = 512 if is_q else 0
                ev_eng = "scalar" if is_q else "vector"
                prog.wait("tensor", slab_ld[sp], slab_tgt[s])
                for el in range(4):
                    h = (4 * s + el) % 16
                    eg = 4 * s + el
                    stp = eg % 2
                    if eg >= 2:
                        prog.wait(ev_eng, qkst_dn[stp], 16 * (eg // 2))
                    for tg in range(ntg):
                        gp = qkmm.n % 2
                        if qkmm.n >= 2:
                            prog.wait("tensor", qkev, qkmm.n - 1)
                        for i in range(NI):
                            prog.op("tensor", lambda e, sp=sp, el=el, tg=tg, i=i, gp=gp, to=toff: e.matmul(
                                qk_ps[gp][:],
                                wsl[sp][:, 512 * i + 128 * el: 512 * i + 128 * (el + 1)],
                                xnt[:, TA * i + to + 512 * tg: TA * i + to + 512 * (tg + 1)],
                                start=(i == 0), stop=(i == NI - 1)),
                                qkmm if i == NI - 1 else None)
                        prog.wait(ev_eng, qkmm, qkmm.n)
                        if is_q:
                            prog.op("scalar", lambda e, tg=tg, stp=stp, gp=gp: e.activation(
                                qkst[stp][:, 512 * tg:512 * (tg + 1)], qk_ps[gp][:],
                                AF.Copy, scale=SCALE), qkev)
                        else:
                            prog.op("vector", lambda e, tg=tg, stp=stp, gp=gp: e.tensor_copy(
                                qkst[stp][:, 512 * tg:512 * (tg + 1)], qk_ps[gp][:]), qkev)
                    prog.wait("sync", qkev, qkev.n)
                    if is_q:
                        prog.dma("sync", QT[h], qkst[stp][:, 0:TC], qkst_dn[stp])
                    else:
                        prog.dma("sync", KT[h], qkst[stp][:, 0:TA], qkst_dn[stp])
                slab_free[s] = (qkmm, qkmm.n)
                if s + 2 < 12:
                    load_slab(s + 2)
            for s in range(8, 12):
                sp = s % 2
                g = s - 8
                prog.wait("tensor", slab_ld[sp], slab_tgt[s])
                for tq in range(5):
                    qg = 5 * g + tq
                    vp = qg % 2
                    if qg >= 2:
                        prog.wait("vector", vst_dn[vp], 16 * (qg // 2))
                    for a in range(4):
                        tau = 4 * tq + a
                        gp = vmm.n % 2
                        if vmm.n >= 2:
                            prog.wait("tensor", vev, vmm.n - 1)
                        for i in range(NI):
                            prog.op("tensor", lambda e, sp=sp, tau=tau, i=i, gp=gp: e.matmul(
                                qk_ps[gp][:],
                                xnt[:, TA * i + 128 * tau: TA * i + 128 * (tau + 1)],
                                wsl[sp][:, 512 * i:512 * (i + 1)],
                                start=(i == 0), stop=(i == NI - 1)),
                                vmm if i == NI - 1 else None)
                        prog.wait("vector", vmm, vmm.n)
                        prog.op("vector", lambda e, a=a, vp=vp, gp=gp: e.tensor_copy(
                            vstb[vp][:, 512 * a:512 * (a + 1)], qk_ps[gp][:]), vev)
                    prog.wait("sync", vev, vev.n)
                    prog.dma("sync",
                             VN[4 * tq:4 * tq + 4, :, 512 * g:512 * (g + 1)].rearrange(
                                 "a p e -> p a e"),
                             vstb[vp][:].rearrange("p (a e) -> p a e", a=4), vst_dn[vp])
                slab_free[s] = (vmm, vmm.n)
                if s + 2 < 12:
                    load_slab(s + 2)
            for sd in qkst_dn + vst_dn:
                prog.wait("sync", sd, sd.n)

        s23.close()
        s123.close()
        prog.barrier(bar)
        if phases < 4:
            finish(prog, nc)
            return nc

        with ExitStack() as p4:
            E4 = p4.enter_context
            qtb = [E4(nc.sbuf_tensor(f"qtb{p}", [128, H * 512], BF16)) for p in range(2)]
            ktw = E4(nc.sbuf_tensor("ktw", [128, H * 3 * 512], BF16))
            vnw = E4(nc.sbuf_tensor("vnw", [128, 3 * 4 * E], BF16))
            pts = [E4(nc.sbuf_tensor(f"pts{p}", [128, 8 * 1024], BF16)) for p in range(2)]
            rlb = [E4(nc.sbuf_tensor(f"rlb{p}", [128, 1024], F32)) for p in range(2)]
            rltmp = E4(nc.sbuf_tensor("rltmp", [128, 1024], F32))
            otst = E4(nc.sbuf_tensor("otst", [128, H * 512], BF16))
            otraw = [E4(nc.sbuf_tensor(f"otraw{p}", [128, 1024], BF16)) for p in range(2)]
            s4 = [E4(nc.psum_tensor(f"s4_{p}", [128, 1024], F32)) for p in range(2)]
            ob2 = E4(nc.psum_tensor("ob2", [128, 1024], F32))
            lb2 = E4(nc.psum_tensor("lb2", [128, 1024], F32))

            kv_ld = [SM("kv_ld0"), SM("kv_ld1"), SM("kv_ld2")]
            kt2_ld = [SM("kt2_ld0"), SM("kt2_ld1"), SM("kt2_ld2")]
            vn_ld = [SM("vn_ld0"), SM("vn_ld1"), SM("vn_ld2")]
            qt_ld = [SM("qt_ld0"), SM("qt_ld1")]
            smm = SM("smm")
            expd = SM("expd")
            mskd = SM("mskd")
            avl = SM("avl")
            lnd = SM("lnd")
            rld = SM("rld")
            oraw = SM("oraw")
            oev = SM("oev")
            ot_st = SM("ot_st")

            for sd in qkst_dn + vst_dn:
                prog.wait("sync", sd, sd.n)

            def load_kv(gc):
                sl = gc % 3
                ktw4 = ktw[:].rearrange("p (h r t) -> p h r t", h=H, r=3)
                prog.dma("sync", ktw4[:, 0:8, sl, :],
                         KT[0:8, :, 512 * gc:512 * (gc + 1)].rearrange("h p t -> p h t"),
                         kv_ld[sl])
                prog.dma("scalar", ktw4[:, 8:16, sl, :],
                         KT[8:16, :, 512 * gc:512 * (gc + 1)].rearrange("h p t -> p h t"),
                         kt2_ld[sl])
                prog.dma("gpsimd",
                         vnw[:].rearrange("p (r a e) -> p r a e", r=3, a=4)[:, sl],
                         VN[4 * gc:4 * (gc + 1)].rearrange("a p e -> p a e"),
                         vn_ld[sl])
                return kv_ld[sl].n

            def load_qt(c):
                prog.dma("gpsimd", qtb[c % 2][:].rearrange("p (h t) -> p h t", h=H),
                         QT[:, :, 512 * c:512 * (c + 1)].rearrange("h p t -> p h t"),
                         qt_ld[c % 2])
                return qt_ld[c % 2].n

            kv_rdy = {0: load_kv(0), 1: load_kv(1)}
            qt_rdy = {0: load_qt(0)}

            for cc in range(4):
                gc = cc + 1
                if cc < 3:
                    if cc >= 1:
                        prog.wait("sync", avl, cc * 8)
                        prog.wait("gpsimd", avl, cc * 8)
                        prog.wait("scalar", avl, cc * 8)
                    kv_rdy[gc + 1] = load_kv(gc + 1)
                    qt_rdy[cc + 1] = load_qt(cc + 1)
                for sml in (kv_ld, kt2_ld, vn_ld):
                    prog.wait("tensor", sml[(gc - 1) % 3], kv_rdy[gc - 1])
                    prog.wait("tensor", sml[gc % 3], kv_rdy[gc])
                prog.wait("tensor", qt_ld[cc % 2], qt_rdy[cc])
                for h2 in range(H // 2):
                    gp2 = cc * 8 + h2
                    pp = h2 % 2
                    if gp2 >= 2:
                        prog.wait("scalar", avl, gp2 - 1)
                    jobs = [("own", jj, 128 * jj, 512, gc % 3, jj) for jj in range(4)] + \
                           [("prev", jj, 0, 128 * (jj + 1), (gc - 1) % 3, jj) for jj in range(4)]

                    def emit_sjob(idx, h2=h2, gp2=gp2, cc=cc, jobs=jobs, pp=pp):
                        gq = gp2 * 8 + idx
                        kind, jj, qlo, qhi, sl, jt = jobs[idx]
                        mend = (kind == "prev" and cc == 0)
                        if gq >= 2:
                            prog.wait("tensor", expd, gq - 1)
                        for a in range(2):
                            h = 2 * h2 + a
                            last = (a == 1) and not mend
                            prog.op("tensor", lambda e, h=h, jj=jj, qlo=qlo, qhi=qhi, sl=sl,
                                    gq=gq, a=a, cp=cc % 2, mend=mend: e.matmul(
                                s4[gq % 2][:, 512 * a + qlo:512 * a + qhi],
                                ktw[:, 1536 * h + 512 * sl + 128 * jj: 1536 * h + 512 * sl + 128 * (jj + 1)],
                                qtb[cp][:, 512 * h + qlo: 512 * h + qhi],
                                start=True, stop=not mend),
                                smm if last else None)
                            if mend:
                                prog.op("tensor", lambda e, qlo=qlo, qhi=qhi, gq=gq, a=a: e.matmul(
                                    s4[gq % 2][:, 512 * a + qlo:512 * a + qhi], m0_sb[0:1, :],
                                    ones_sb[0:1, 0:qhi - qlo],
                                    start=False, stop=True),
                                    smm if a == 1 else None)
                        prog.wait("scalar", smm, smm.n)
                        prog.op("scalar", lambda e, idx=idx, qlo=qlo, qhi=qhi, gq=gq, pp=pp: e.activation(
                            pts[pp][:].rearrange("p (s a w) -> p s a w", s=8, a=2)[:, idx, :, qlo:qhi],
                            s4[gq % 2][:].rearrange("p (a w) -> p a w", a=2)[:, :, qlo:qhi],
                            AF.Exp), expd)
                        prog.wait("vector", expd, expd.n)
                        tri = triO01 if kind == "own" else triP01
                        base = 8 * idx + jj
                        dst = pts[pp][:].rearrange("p (x w) -> p x w", w=128)[
                            :, base:base + 5:4, :]
                        prog.op("vector", lambda e, d=dst, m=tri: e.tensor_tensor(
                            d, d, m[:].rearrange("p (a w) -> p a w", a=2), ALU.mult), mskd)

                    def emit_avl(idx, h2=h2, gp2=gp2, jobs=jobs, pp=pp):
                        kind, jj, qlo, qhi, sl, jt = jobs[idx]
                        prog.wait("tensor", mskd, gp2 * 8 + idx + 1)
                        for a in range(2):
                            h = 2 * h2 + a
                            prog.op("tensor", lambda e, h=h, sl=sl, jt=jt, idx=idx, a=a,
                                    qlo=qlo, qhi=qhi, pp=pp: e.matmul(
                                ob2[:, 512 * a + qlo:512 * a + qhi],
                                vnw[:, 8192 * sl + 2048 * jt + 128 * h: 8192 * sl + 2048 * jt + 128 * (h + 1)],
                                pts[pp][:, 1024 * idx + 512 * a + qlo: 1024 * idx + 512 * a + qhi],
                                start=(idx == 0), stop=(idx == 7)))
                        for a in range(2):
                            prog.op("tensor", lambda e, idx=idx, a=a, qlo=qlo, qhi=qhi, pp=pp: e.matmul(
                                lb2[:, 512 * a + qlo:512 * a + qhi], ones_sb[:, 0:128],
                                pts[pp][:, 1024 * idx + 512 * a + qlo: 1024 * idx + 512 * a + qhi],
                                start=(idx == 0), stop=(idx == 7)),
                                avl if idx == 7 and a == 1 else None)

                    emit_sjob(0)
                    emit_sjob(1)
                    for idx in range(8):
                        if idx + 2 < 8:
                            emit_sjob(idx + 2)
                        if idx == 0 and gp2 >= 1:
                            prog.wait("tensor", oraw, gp2)
                            prog.wait("tensor", lnd, gp2)
                        emit_avl(idx)
                    prog.wait("vector", avl, avl.n)
                    prog.op("vector", lambda e, pp=pp: e.tensor_copy(
                        otraw[pp][:], ob2[:]), oraw)
                    prog.wait("scalar", avl, avl.n)
                    prog.op("scalar", lambda e: e.activation(
                        rltmp[:], lb2[:], AF.Ln), lnd)
                    if gp2 >= 2:
                        prog.wait("scalar", oev, gp2 - 1)
                    t = prog.op("scalar", lambda e, pp=pp: e.activation(
                        rlb[pp][:], rltmp[:], AF.Exp, scale=-1.0), rld)
                    prog.wait("vector", rld, t)
                    if cc >= 1 and h2 == 0:
                        prog.wait("vector", ot_st, 16 * cc)
                    prog.op("vector", lambda e, h2=h2, pp=pp: e.tensor_tensor(
                        otst[:, 1024 * h2:1024 * (h2 + 1)], otraw[pp][:], rlb[pp][:],
                        ALU.mult), oev)
                prog.wait("scalar", oev, oev.n)
                prog.dma("scalar", OT[:, :, 512 * cc:512 * (cc + 1)].rearrange("h p t -> p h t"),
                         otst[:].rearrange("p (h t) -> p h t", h=H), ot_st)
            prog.wait("scalar", ot_st, ot_st.n)

        prog.barrier(bar)
        if phases < 5:
            finish(prog, nc)
            return nc

        with ExitStack() as p5:
            E5 = p5.enter_context
            wout_sb = E5(nc.sbuf_tensor("wout_sb", [128, NI * E], BF16))
            otc = [E5(nc.sbuf_tensor(f"otc{p}", [128, H * 128], BF16)) for p in range(2)]
            gt = [E5(nc.sbuf_tensor(f"gt{p}", [128, E], BF16)) for p in range(2)]
            xt = [E5(nc.sbuf_tensor(f"xt{p}", [128, E], F32)) for p in range(2)]
            sfin = [E5(nc.sbuf_tensor(f"sfin{p}", [128, 512], F32)) for p in range(2)]
            obf = [E5(nc.sbuf_tensor(f"obf{p}", [128, E], F32)) for p in range(4)]
            s5_ps = [E5(nc.psum_tensor(f"s5_ps{p}", [128, 512], F32)) for p in range(2)]

            w5_ld = [SM(f"w5_ld{s}") for s in range(4)]
            t5_ld = [SM("t5_ld0"), SM("t5_ld1")]
            x5_ld = [SM("x5_ld0"), SM("x5_ld1")]
            s5mm = SM("s5mm")
            findn = SM("findn")
            outst = [SM(f"outst{p}") for p in range(4)]
            v5_self = SM("v5_self")

            for sd in g_st:
                prog.wait("sync", sd, sd.n)

            for s in range(4):
                prog.dma("gpsimd",
                         wout_sb[:].rearrange("p (i e) -> p i e", i=NI)[:, :, 512 * s:512 * (s + 1)],
                         woutT[:, 512 * s:512 * (s + 1)].rearrange("(i p) e -> p i e", p=128),
                         w5_ld[s])

            def load_tt(tt):
                tp = tt % 2
                prog.wait("sync", ot_st, 16 * (tt // 4 + 1))
                if tt >= 2:
                    prog.wait("sync", findn, 4 * (tt - 1))
                prog.dma("sync", otc[tp][:].rearrange("p (h t) -> p h t", h=H),
                         OT[:, :, 128 * tt:128 * (tt + 1)].rearrange("h p t -> p h t"),
                         t5_ld[tp])
                prog.dma("sync", gt[tp][:], G[tt], t5_ld[tp])
                if tt >= 2:
                    prog.wait("scalar", findn, 4 * (tt - 1))
                prog.dma("scalar", xt[tp][:],
                         xh[512 + 128 * tt: 512 + 128 * (tt + 1), :], x5_ld[tp])
                return t5_ld[tp].n

            rdy = {0: load_tt(0)}
            mmq = 0
            for tt in range(NTC):
                tp = tt % 2
                if tt < NTC - 1:
                    rdy[tt + 1] = load_tt(tt + 1)
                prog.wait("tensor", t5_ld[tp], rdy[tt])
                prog.wait("tensor", x5_ld[tp], 16 * (tt // 2 + 1))
                if tt >= 4:
                    prog.wait("vector", outst[tt % 4], 16 * (tt // 4))
                for og in range(4):
                    gp = mmq % 2
                    if tt == 0:
                        prog.wait("tensor", w5_ld[og], 16)
                    if mmq >= 2:
                        prog.wait("tensor", findn, mmq - 1)
                    for hh in range(H):
                        prog.op("tensor", lambda e, tp=tp, og=og, hh=hh, gp=gp: e.matmul(
                            s5_ps[gp][:],
                            otc[tp][:, 128 * hh:128 * (hh + 1)],
                            wout_sb[:, E * hh + 512 * og: E * hh + 512 * (og + 1)],
                            start=(hh == 0), stop=(hh == H - 1)),
                            s5mm if hh == H - 1 else None)
                    mmq += 1
                    prog.wait("vector", s5mm, mmq)
                    t = prog.op("vector", lambda e, tp=tp, og=og, gp=gp: e.tensor_tensor(
                        sfin[gp][:], s5_ps[gp][:], gt[tp][:, 512 * og:512 * (og + 1)],
                        ALU.mult), v5_self)
                    prog.wait("vector", v5_self, t)
                    prog.op("vector", lambda e, tt=tt, og=og, gp=gp, tp=tp: e.tensor_tensor(
                        obf[tt % 4][:, 512 * og:512 * (og + 1)], sfin[gp][:],
                        xt[tp][:, 512 * og:512 * (og + 1)], ALU.add), findn)
                prog.wait("gpsimd", findn, findn.n)
                prog.dma("gpsimd", out[128 * tt:128 * (tt + 1), :], obf[tt % 4][:],
                         outst[tt % 4])
            for sd in outst:
                prog.wait("sync", sd, sd.n)

        finish(prog, nc)
    return nc


def finish(prog, nc):
    with nc.Block() as block:
        for name in ENGS:
            deco = getattr(block, name)

            def runner(eng, name=name):
                for fn in prog.streams[name]:
                    fn(eng)

            deco(runner)



def _consts():
    ident = np.eye(128, dtype=ml_dtypes.bfloat16)
    ones = np.ones((128, 512), dtype=ml_dtypes.bfloat16)
    p = np.arange(128)[:, None]
    f = np.arange(128)[None, :]
    triP = np.where(p >= f, 1.0, 0.0).astype(ml_dtypes.bfloat16)
    triO = np.where(p <= f, 1.0, 0.0).astype(ml_dtypes.bfloat16)
    triP = np.ascontiguousarray(np.concatenate([triP, triP], axis=1))
    triO = np.ascontiguousarray(np.concatenate([triO, triO], axis=1))
    return ident, ones, triP, triO


def make_in_maps(x, wqkv, wout, wffn, bias):
    ident, ones, triP, triO = _consts()
    wqkvT = np.ascontiguousarray(np.asarray(wqkv, dtype=np.float32).T)
    woutT = np.ascontiguousarray(np.asarray(wout, dtype=np.float32).T)
    wffnT = np.ascontiguousarray(np.asarray(wffn, dtype=np.float32).T)
    bias1 = np.ascontiguousarray(np.asarray(bias, dtype=np.float32).reshape(1, E))
    in_maps = []
    for core in range(8):
        b, j = core // 4, core % 4
        lo = 2048 * j
        if j == 0:
            halo = np.zeros((C, E), dtype=np.float32)
            m0 = np.full((1, 128), NEG, dtype=ml_dtypes.bfloat16)
        else:
            halo = np.asarray(x[b, lo - C:lo], dtype=np.float32)
            m0 = np.zeros((1, 128), dtype=ml_dtypes.bfloat16)
        xhc = np.ascontiguousarray(
            np.concatenate([halo, np.asarray(x[b, lo:lo + TC], dtype=np.float32)], 0))
        in_maps.append(dict(
            xh=xhc, wqkvT=wqkvT, woutT=woutT, wffnT=wffnT, bias1=bias1,
            identN=ident, ones_in=ones, triP_in=triP, triO_in=triO, m0row=m0))
    return in_maps


def kernel(L0_attn_x, L0_attn_w_qkv, L0_attn_w_out, L0_ffn_weight, L0_ffn_bias):
    x = np.asarray(L0_attn_x)
    in_maps = make_in_maps(x, L0_attn_w_qkv, L0_attn_w_out, L0_ffn_weight, L0_ffn_bias)
    nc = build()
    res = run_bass_kernel_spmd(nc, in_maps, core_ids=list(range(8)))
    outa = np.empty((2, 8192, 2048), dtype=np.float32)
    for core in range(8):
        b, j = core // 4, core % 4
        outa[b, 2048 * j:2048 * (j + 1)] = res.results[core]["out"]
    return outa
